# version 29
# baseline (speedup 1.0000x reference)
import numpy as np
import ml_dtypes
import concourse.bass as bass
import concourse.mybir as mybir
from concourse.bass_utils import run_bass_kernel_spmd
from contextlib import ExitStack

F32 = mybir.dt.float32
BF16 = mybir.dt.bfloat16
U8 = mybir.dt.uint8
AF = mybir.ActivationFunctionType
ALU = mybir.AluOpType

D = 1024
NFC = 8
NH = 16
DH = 64
TSEQ = 2048
NSEQ = 2
TC = 256
CC = NSEQ * TC
NWIN = 4
NCHUNK = TSEQ // TC
NCORE = 8
EPS = 1e-5

ENGINES = ("t", "v", "s", "g", "y")


class Tok:
    __slots__ = ("eng", "cnt")

    def __init__(self, eng, cnt):
        self.eng = eng
        self.cnt = cnt


class Buf:
    __slots__ = ("w", "rs")

    def __init__(self):
        self.w = None
        self.rs = []


class Ctx:
    def __init__(self):
        self.steps = {e: [] for e in ENGINES}
        self.cnt = {e: 0 for e in ENGINES}
        self.async_cnts = {e: set() for e in ENGINES}

    NY = 8

    def emit(self, eng, fn, reads=(), writes=(), inc=1, sync_self=False):
        if eng in ("s", "v", "g") or inc >= 16:
            sync_self = True
        deps = {}

        def add(tok):
            if tok is None:
                return
            if tok.eng == eng and tok.cnt not in self.async_cnts[eng]:
                return
            if True:
                if tok.eng == "y":
                    slot = ((tok.cnt // 16) - 1) % Ctx.NY
                    key = ("y", slot)
                else:
                    key = tok.eng
                if deps.get(key, 0) < tok.cnt:
                    deps[key] = tok.cnt
        for b in reads:
            add(b.w)
        for b in writes:
            add(b.w)
            for r in b.rs:
                add(r)
        self.steps[eng].append((fn, deps, inc))
        self.cnt[eng] += inc
        tok = Tok(eng, self.cnt[eng])
        if sync_self:
            self.async_cnts[eng].add(tok.cnt)
        for b in reads:
            b.rs.append(tok)
        for b in writes:
            b.w = tok
            b.rs = []
        return tok

    def realize(self, blk, sems, ysems):
        NY = len(ysems)
        assert NY == Ctx.NY
        waited = {e: set() for e in ENGINES}
        for e in ENGINES:
            for (fn, deps, inc) in self.steps[e]:
                for de, dc in deps.items():
                    waited[de[0] if isinstance(de, tuple) else de].add(dc)
        semval = {}
        for e in ENGINES:
            if e == "y":
                continue
            vals = {}
            sv = 0
            cnt = 0
            incs = []
            for (fn, deps, inc) in self.steps[e]:
                assert inc == 1, f"non-DMA inc on {e}"
                cnt += inc
                do_inc = cnt in waited[e]
                if do_inc:
                    sv += 1
                incs.append(1 if do_inc else 0)
                vals[cnt] = sv
            semval[e] = (vals, incs)
        ymap = {}
        yinfo = []
        cnt = 0
        for i, (fn, deps, inc) in enumerate(self.steps["y"]):
            assert inc == 16, "y engine must only carry DMAs"
            cnt += inc
            s = i % NY
            val = 16 * (i // NY + 1)
            ymap[cnt] = (s, val)
            yinfo.append((s, val))

        def wait_for_dep(eng, de, dc, wsf):
            if isinstance(de, tuple):
                s, val = ymap[dc]
                if val > wsf[("y", s)]:
                    eng.wait_ge(ysems[s], val)
                    wsf[("y", s)] = val
            else:
                wv = semval[de][0][dc]
                if wv > wsf[de]:
                    eng.wait_ge(sems[de], wv)
                    wsf[de] = wv

        def body_for(ename):
            def body(eng):
                wsf = {e: 0 for e in ENGINES if e != "y"}
                wsf[ename] = 0 if ename != "y" else wsf.get(ename, 0)
                for s in range(NY):
                    wsf[("y", s)] = 0
                for i, (fn, deps, inc) in enumerate(self.steps[ename]):
                    for de in sorted(deps, key=str):
                        wait_for_dep(eng, de, deps[de], wsf)
                    if ename == "y":
                        s, val = yinfo[i]
                        if val > 16 and val - 16 > wsf[("y", s)]:
                            eng.wait_ge(ysems[s], val - 16)
                            wsf[("y", s)] = val - 16
                    inst = fn(eng)
                    if ename == "y":
                        s, val = yinfo[i]
                        inst.then_inc(ysems[s], 16)
                    elif semval[ename][1][i]:
                        inst.then_inc(sems[ename], 1)
                if ename == "y":
                    for s in range(NY):
                        n = sum(1 for (si, _) in yinfo if si == s)
                        if n:
                            eng.wait_ge(ysems[s], 16 * n)
            return body
        blk.sync(body_for("y"))
        blk.vector(body_for("v"))
        blk.scalar(body_for("s"))
        blk.tensor(body_for("t"))
        blk.gpsimd(body_for("g"))


PP_TMR, PP_TMK, PP_TMV, PP_TMW = 0, 8, 16, 24
PP_MUR, PP_MUK, PP_U = 32, 40, 48
PP_LNW, PP_LNB = 56, 57
PP_OMTV = 58
PP_EPS = 66
PP_COLS = 67

STREAMED = ("wv", "wo", "wr2", "wk2", "wv2")


def host_consts():
    bf = ml_dtypes.bfloat16
    c = {}
    c["identity"] = np.eye(128).astype(bf)
    bd = np.zeros((128, 128), np.float32)
    bd[0:64, 0:64] = 1.0
    bd[64:128, 64:128] = 1.0
    c["bdones"] = bd.astype(bf)
    hi = np.zeros((128, 2), np.float32)
    hi[0:64, 0] = 1.0
    hi[64:128, 1] = 1.0
    c["headind"] = hi.astype(bf)
    c["triu4"] = np.ascontiguousarray(
        np.tile(np.triu(np.ones((128, 128), np.uint8), 1), (1, 4)))
    c["idmask4"] = np.ascontiguousarray(
        np.tile(np.eye(128, dtype=np.float32), (1, 4))).astype(bf)
    rs = np.ones((1, CC), np.float32)
    rs[0, ::128] = 0.0
    c["reset"] = np.ascontiguousarray(np.tile(rs, (128, 1))).astype(bf)
    rp = ((np.arange(CC) % 128) + 1).astype(np.float32) * 1e-4
    c["ramp"] = np.ascontiguousarray(np.tile(rp[None, :], (128, 1))).astype(bf)
    return c


CONST_DTYPES = dict(identity=BF16, bdones=BF16, headind=BF16, triu4=U8,
                    idmask4=BF16, reset=BF16, ramp=BF16)


def build_pp(ins):
    cols = []
    for nm in ("tm_r", "tm_k", "tm_v", "tm_w"):
        cols.append(np.asarray(ins["L0_attn_" + nm], np.float32).reshape(NFC, 128).T)
    cols.append(np.asarray(ins["L0_ffn_mu_r"], np.float32).reshape(NFC, 128).T)
    cols.append(np.asarray(ins["L0_ffn_mu_k"], np.float32).reshape(NFC, 128).T)
    cols.append(np.asarray(ins["L0_attn_u"], np.float32).reshape(NFC, 128).T)
    cols.append(np.tile(np.asarray(ins["L0_attn_ln_w"], np.float32), 2)[:, None])
    cols.append(np.tile(np.asarray(ins["L0_attn_ln_b"], np.float32), 2)[:, None])
    cols.append(1.0 - np.asarray(ins["L0_attn_tm_v"], np.float32).reshape(NFC, 128).T)
    cols.append(np.full((128, 1), EPS, np.float32))
    return np.ascontiguousarray(np.concatenate(cols, axis=1).astype(np.float32))


def build_program(nchunk=NCHUNK, dbg=False, stop_after=None):
    nc = bass.Bass("TRN2", target_bir_lowering=False, debug=False,
                   dynamic_dma_scratch_size=8192)
    ctx = Ctx()
    E = ctx.emit

    xd = nc.dram_tensor("x", [NSEQ, TSEQ, D], F32, kind="ExternalInput").ap()
    od = nc.dram_tensor("out", [NSEQ, TSEQ, D], F32, kind="ExternalOutput").ap()
    wd = {nm: nc.dram_tensor(nm, [D, D], F32, kind="ExternalInput").ap()
          for nm in ("wr", "wk", "ww") + STREAMED}
    ppd = nc.dram_tensor("pp", [128, PP_COLS], F32, kind="ExternalInput").ap()
    cd = {}
    cshape = dict(identity=(128, 128), bdones=(128, 128), headind=(128, 2),
                  triu4=(128, 512), idmask4=(128, 512),
                  reset=(128, CC), ramp=(128, CC))
    for nm, shp in cshape.items():
        cd[nm] = nc.dram_tensor("c_" + nm, list(shp), CONST_DTYPES[nm],
                                kind="ExternalInput").ap()
    wbf = {nm: nc.dram_tensor("wbf_" + nm, [D, D], BF16, kind="Internal").ap()
           for nm in STREAMED}
    wwt_d = {nm: nc.dram_tensor("wbf_t" + nm, [NFC * D, 128], BF16,
                                kind="Internal").ap()
             for nm in ("ww", "wk", "wr")}
    dbg_outs = {}

    def dout(name, shape, dtype=F32):
        t = nc.dram_tensor("dbg_" + name, list(shape), dtype,
                           kind="ExternalOutput").ap()
        dbg_outs[name] = t
        return t

    def S(name, shape, dtype):
        return nc.alloc_sbuf_tensor(name, shape, dtype)

    wwt = S("wwt", [128, 3, 2, NFC, 128], BF16)
    wstr = S("wstr", [128, NFC, D], BF16)
    xin = S("xin", [128, 4, 2, D], F32)
    xhf = S("xhf", [128, NFC, 2, 258], BF16)
    dbuf = S("dbuf", [128, NFC, CC], BF16)
    xw = S("xw", [128, NFC, CC], BF16)
    xk = S("xk", [128, NFC, CC], BF16)
    xr = S("xr", [128, NFC, CC], BF16)
    scrF = [S(f"scrF{i}", [128, D], F32) for i in range(4)]
    cplus = [S(f"cplus{i}", [128, CC], F32) for i in range(2)]
    ekbs = S("ekbs", [128, 2, D], BF16)
    ekus = S("ekus", [128, 2, D], BF16)
    tbs = S("tbs", [128, 2, D], BF16)
    khs = S("khs", [128, 2, D], BF16)
    gsq = S("gsq", [128, 2, D], BF16)
    k2s = S("k2s", [128, D], BF16)
    rbuf = S("rbuf", [128, NFC, CC], BF16)
    qt = S("qt", [128, NFC, CC], BF16)
    ktil = S("ktil", [128, NFC, CC], BF16)
    vtok = S("vtok", [128, NWIN, D], BF16)
    khT = S("khT", [128, 2, D], BF16)
    atil = S("atil", [128, 4, 512], BF16)
    osb = S("osb", [128, 2, D], BF16)
    p1b = S("p1b", [128, 2, D], BF16)
    rstdb = S("rstdb", [128, D], BF16)
    kact = S("kact", [128, NFC, CC], BF16)
    stt_s = S("stt_s", [128, NSEQ, NFC, DH], F32)
    stb = S("stb", [128, NFC * DH], BF16)
    bts = S("bts", [128, NWIN * NH], BF16)
    bnb = S("bnb", [128, 8, 16], F32)
    mvb = S("mvb", [128, 8, 2], F32)
    lnr = S("lnr", [128, 8, 2], F32)
    bp = S("bp", [128, NFC * NWIN], F32)
    nbp = S("nbp", [128, NFC * NWIN], F32)
    enb = S("enb", [128, NFC * NWIN], F32)
    egb = S("egb", [128, NFC * NWIN], F32)
    bx1 = S("bx1", [128, NFC, 2], BF16)
    bx2 = S("bx2", [128, NFC, 2], BF16)
    ppb = S("ppb", [128, PP_COLS], F32)
    cb = {nm: S("cb_" + nm, list(shp), CONST_DTYPES[nm])
          for nm, shp in cshape.items()}

    PJ0 = nc.alloc_psum_tensor("PJ0", [128, 512], F32)
    PJ1 = nc.alloc_psum_tensor("PJ1", [128, 512], F32)
    PT0 = nc.alloc_psum_tensor("PT0", [128, 512], F32)
    PT1 = nc.alloc_psum_tensor("PT1", [128, 512], F32)
    PA0 = nc.alloc_psum_tensor("PA0", [128, 512], F32)
    PA1 = nc.alloc_psum_tensor("PA1", [128, 512], F32)
    POA = nc.alloc_psum_tensor("POA", [128, 512], F32)
    POB = nc.alloc_psum_tensor("POB", [128, 512], F32)

    Bm = {}

    def bf(*key):
        if key not in Bm:
            Bm[key] = Buf()
        return Bm[key]

    bPJ0, bPJ1 = bf("PJ0"), bf("PJ1")
    bPT0, bPT1 = bf("PT0"), bf("PT1")
    bPA0, bPA1, bPOA, bPOB = bf("PA0"), bf("PA1"), bf("POA"), bf("POB")

    ppc = lambda col: ppb[:, col:col + 1]

    for nm in cshape:
        E("y", (lambda e, d=cd[nm], s=cb[nm]: e.dma_start(out=s[:, :], in_=d)),
          writes=[bf("c", nm)], inc=16)
    E("y", lambda e: e.dma_start(out=ppb[:, :], in_=ppd), writes=[bf("pp")], inc=16)

    for pt, ptb in ((PJ0, bPJ0), (PJ1, bPJ1), (PT0, bPT0), (PT1, bPT1),
                    (PA0, bPA0), (PA1, bPA1), (POA, bPOA), (POB, bPOB)):
        E("v", (lambda e, p=pt: e.memset(p[:, :], 0.0)), writes=[ptb])
    E("v", lambda e: e.memset(stt_s[:, :, :, :], 0.0), writes=[bf("state")])
    E("v", lambda e: e.memset(bx1[:, :, :], 0.0), writes=[bf("bx1")])
    E("v", lambda e: e.memset(bx2[:, :, :], 0.0), writes=[bf("bx2")])

    cast_i = [0]
    cast_q = []

    def cast_bounce(i, tiled):
        j = i % 4
        if tiled:
            return (scrF[j][:, :],
                    [bf("xinc", 4 + j), bf("scrF", j, 0), bf("scrF", j, 1)])
        return (xin[:, 2 + j // 2, j % 2, :], [bf("xinc", j)])

    def cast_in(nm, kf, tiled):
        i = cast_i[0]
        cast_i[0] += 1
        ap, keys = cast_bounce(i, tiled)
        E("y", (lambda e, d=wd[nm], k=kf, a=ap:
                e.dma_start(out=a, in_=d[k * 128:(k + 1) * 128, :])),
          writes=keys, inc=16)

    cast_j = [0]

    def cast_fin(nm, kf, tiled):
        i = cast_j[0]
        cast_j[0] += 1
        ap, keys = cast_bounce(i, tiled)
        if tiled:
            E("v", (lambda e, k=kf, a=ap:
                    e.tensor_copy(wstr[:, k, :], a)),
              reads=keys, writes=[bf("wstr", kf)])
            E("y", (lambda e, d=wwt_d[nm], k=kf:
                    e.dma_start(
                        out=d.rearrange("(mf kf p) n -> p mf kf n",
                                        mf=NFC, kf=NFC)[:, :, k, :],
                        in_=wstr[:, k, :].rearrange("p (mf n) -> p mf n",
                                                    mf=NFC))),
              reads=[bf("wstr", kf)], writes=[bf("wbfd", "t" + nm, kf)],
              inc=16)
        else:
            j = kf % 4
            kv = kact[:, 2 * j:2 * j + 2, :].rearrange("p a c -> p (a c)")
            E("v", (lambda e, t=kv, a=ap:
                    e.tensor_copy(t, a)),
              reads=keys, writes=[bf("kact", 2 * j)])
            E("y", (lambda e, d=wbf[nm], k=kf, t=kv:
                    e.dma_start(out=d[k * 128:(k + 1) * 128, :], in_=t)),
              reads=[bf("kact", 2 * j)], writes=[bf("wbfd", nm, kf)], inc=16)

    def emit_casts(units):
        for idx in range(min(4, len(units))):
            cast_in(*units[idx])
        for idx, u in enumerate(units):
            cast_fin(*u)
            if idx + 4 < len(units):
                cast_in(*units[idx + 4])

    def drain_casts(n):
        batch = cast_q[:n]
        del cast_q[:n]
        if batch:
            emit_casts(batch)

    def wbfd_reads(nm):
        return [bf("wbfd", nm, k) for k in range(NFC)]

    def xin_slot(ci, seq):
        return (2 * ci + seq) % 4

    def emit_ln_stats(get_src_buf, get_src_ap, w, soff=0):
        seq, half = divmod(w, 2)
        p_ = w % 4 + soff
        src_ap = get_src_ap(seq, half)
        srcb = get_src_buf(seq)
        E("v", (lambda e, a=src_ap, p=p_:
                e.bn_stats(bnb[:, p, 0:6], a[:, 0:512])),
          reads=[srcb], writes=[bf("bnb", p_)], sync_self=True)
        E("v", (lambda e, a=src_ap, p=p_:
                e.bn_stats(bnb[:, p, 6:12], a[:, 512:1024])),
          reads=[srcb], writes=[bf("bnb", p_)], sync_self=True)
        E("v", (lambda e, p=p_: e.bn_aggr(mvb[:, p, 0:2], bnb[:, p, 0:12])),
          reads=[bf("bnb", p_)], writes=[bf("mvb", p_)], sync_self=True)
        E("s", (lambda e, p=p_: e.activation(lnr[:, p, 0:1], mvb[:, p, 1:2],
                                      AF.Ln,
                                      bias=ppc(PP_EPS), scale=1.0)),
          reads=[bf("mvb", p_), bf("pp")], writes=[bf("lnr", p_)])
        E("s", (lambda e, p=p_: e.activation(lnr[:, p, 1:2], lnr[:, p, 0:1],
                                      AF.Exp, bias=0.0, scale=-0.5)),
          reads=[bf("lnr", p_)], writes=[bf("lnr", p_)])

    def emit_ln_transpose(get_src_buf, get_src_ap, tag, nstats_done=0, soff=0):
        def xtok(w):
            return (khT[:, 0, :], gsq[:, 0, :], osb[:, 0, :], osb[:, 1, :])[w % 4]

        def xtok_bufs(w):
            return ([bf("khT", 0)], [bf("gsq", 0)],
                    [bf("osb", 0)], [bf("osb", 1)])[w % 4]

        def emit_stats(w):
            seq, half = divmod(w, 2)
            p_ = w % 4 + soff
            src_ap = get_src_ap(seq, half)
            srcb = get_src_buf(seq)
            E("v", (lambda e, a=src_ap, p=p_:
                    e.bn_stats(bnb[:, p, 0:6], a[:, 0:512])),
              reads=[srcb], writes=[bf("bnb", p_)], sync_self=True)
            E("v", (lambda e, a=src_ap, p=p_:
                    e.bn_stats(bnb[:, p, 6:12], a[:, 512:1024])),
              reads=[srcb], writes=[bf("bnb", p_)], sync_self=True)
            E("v", (lambda e, p=p_: e.bn_aggr(mvb[:, p, 0:2], bnb[:, p, 0:12])),
              reads=[bf("bnb", p_)], writes=[bf("mvb", p_)], sync_self=True)
            E("s", (lambda e, p=p_: e.activation(lnr[:, p, 0:1], mvb[:, p, 1:2],
                                          AF.Ln,
                                          bias=ppc(PP_EPS), scale=1.0)),
              reads=[bf("mvb", p_), bf("pp")], writes=[bf("lnr", p_)])
            E("s", (lambda e, p=p_: e.activation(lnr[:, p, 1:2], lnr[:, p, 0:1],
                                          AF.Exp, bias=0.0, scale=-0.5)),
              reads=[bf("lnr", p_)], writes=[bf("lnr", p_)])

        def emit_xtok(w):
            seq, half = divmod(w, 2)
            p_ = w % 4 + soff
            src_ap = get_src_ap(seq, half)
            srcb = get_src_buf(seq)
            E("v", (lambda e, a=src_ap, ww_=w, p=p_:
                    e.tensor_scalar(xtok(ww_), a, mvb[:, p, 0:1],
                                    lnr[:, p, 1:2],
                                    ALU.subtract, ALU.mult)),
              reads=[srcb, bf("mvb", p_), bf("lnr", p_)],
              writes=xtok_bufs(w))

        def emit_trans(w):
            seq, half = divmod(w, 2)
            bks = (((POA, bPOA), (POB, bPOB)) if w % 2 == 0
                   else ((PJ0, bPJ0), (PJ1, bPJ1)))
            for f in range(NFC):
                pj, pjb = bks[0] if f < 4 else bks[1]
                E("t", (lambda e, ff=f, p=pj, ww_=w:
                        e.matmul(p[:, (ff % 4) * 128:(ff % 4) * 128 + 128],
                                 xtok(ww_)[:, ff * 128:(ff + 1) * 128],
                                 cb["identity"][:, :], start=True, stop=True)),
                  reads=xtok_bufs(w) + [bf("c", "identity")],
                  writes=[pjb])
            c0 = 1 + half * 128
            for pj, pjb, f0 in (bks[0] + (0,), bks[1] + (4,)):
                E("s", (lambda e, p=pj, ff=f0, s=seq, c=c0:
                        e.copy(xhf[:, ff:ff + 4, s, c:c + 128], p[:, :]
                               .rearrange("p (a b) -> p a b", a=4))),
                  reads=[pjb], writes=[bf("xhf", f0 + 0), bf("xhf", f0 + 1),
                                       bf("xhf", f0 + 2), bf("xhf", f0 + 3)])

        for w in range(nstats_done, NWIN):
            emit_ln_stats(get_src_buf, get_src_ap, w, soff)
        for w in range(NWIN):
            emit_xtok(w)
            emit_trans(w)

    xcur = lambda f: xhf[:, f, :, 1:257]
    xprv = lambda f: xhf[:, f, :, 0:256]

    def rd_xhf(f):
        return [bf("xhf", f), bf("xhf_b")]

    ntmp = [0]

    def lerp(f, dst, col, db, eng="g"):
        sl_t = ntmp[0] % 2
        ntmp[0] += 1
        E("v", (lambda e, ff=f, c=col, q=sl_t:
                e.tensor_scalar_mul(k2s[:, q * 512:(q + 1) * 512],
                                    dbuf[:, ff, :], ppc(c + ff))),
          reads=[bf("dbuf", f), bf("pp")], writes=[bf("k2s", sl_t)])
        E(eng, (lambda e, ff=f, t=dst, q=sl_t:
                e.tensor_tensor(
                    t[:, ff, :].rearrange("p (s t) -> p s t", s=2),
                    k2s[:, q * 512:(q + 1) * 512]
                    .rearrange("p (s t) -> p s t", s=2),
                    xprv(ff), ALU.add)),
          reads=[bf("k2s", sl_t), bf("pp")] + rd_xhf(f),
          writes=[bf(db, f)])

    def emit_wwt_load(mf):
        par = mf % 2
        for wi, nm in enumerate(("ww", "wk", "wr")):
            E("y", (lambda e, m=mf, p=par, d=wwt_d[nm], w_=wi:
                    e.dma_start(out=wwt[:, w_, p, :, :],
                                in_=d[m * 1024:(m + 1) * 1024, :]
                                .rearrange("(kf p) n -> p kf n", p=128))),
              reads=wbfd_reads("t" + nm), writes=[bf("wwt", wi, par)],
              inc=16)

    xdma_done = set()
    ln1_stats_done = set()

    def emit_prelude_x(ci):
        if ci in xdma_done or ci >= nchunk:
            return
        xdma_done.add(ci)
        t0 = ci * TC
        for seq in range(NSEQ):
            sl = xin_slot(ci, seq)
            wrt = [bf("xin", sl)]
            if sl >= 2:
                wrt += [bf("xinc", 2 * (sl - 2)), bf("xinc", 2 * (sl - 2) + 1)]
            E("y", (lambda e, s=seq, q=sl, t=t0:
                    e.dma_start(out=xin[:, q, :, :],
                                in_=xd[s, t:t + TC, :]
                                .rearrange("(w p) n -> p w n", p=128))),
              writes=wrt, inc=16)

    def emit_prelude(ci):
        t0 = ci * TC

        def stop(tag):
            return False
        emit_prelude_x(ci)

        emit_ln_transpose(
            lambda seq: bf("xin", xin_slot(ci, seq)),
            lambda seq, half: xin[:, xin_slot(ci, seq), half, :], "ln1",
            nstats_done=(4 if ci in ln1_stats_done else 0), soff=4)

        if stop('ln1'):
            return
        E("v", lambda e: e.tensor_copy(xhf[:, :, :, 0:1], bx1[:, :, :].unsqueeze(3)),
          reads=[bf("bx1")],
          writes=[bf("xhf_b")])

    def emit_chunk(ci):
        t0 = ci * TC

        def stop(tag):
            return stop_after is not None and stop_after == tag

        E("v", (lambda e: e.tensor_tensor(
                dbuf[:, :, :].rearrange("p a (s t) -> p a s t", s=2),
                xhf[:, :, :, 1:257], xhf[:, :, :, 0:256], ALU.subtract)),
          reads=[bf("xhf", f) for f in range(NFC)] + [bf("xhf_b")],
          writes=[bf("dbuf", f) for f in range(NFC)])
        for dst, col, db, eng in ((xw, PP_TMW, "xw", "v"),
                                  (xk, PP_TMK, "xk", "g"),
                                  (xr, PP_TMR, "xr", "g")):
            for f in range(NFC):
                lerp(f, dst, col, db, eng)
        for f in range(NFC):
            E("v", (lambda e, ff=f: e.tensor_scalar_mul(
                    dbuf[:, ff, :], dbuf[:, ff, :], ppc(PP_TMV + ff))),
              reads=[bf("pp")], writes=[bf("dbuf", f)])
        E("v", lambda e: e.tensor_copy(bx1[:, :, :], xhf[:, :, :, 256:257].squeeze(3)),
          reads=[bf("xhf", 0), bf("xhf", 1), bf("xhf", 2), bf("xhf", 3),
                 bf("xhf", 4), bf("xhf", 5), bf("xhf", 6), bf("xhf", 7)],
          writes=[bf("bx1")])

        if stop('lerp'):
            return
        drain_casts(8)
        def emit_khs(b):
            E("v", (lambda e, bb=b: e.tensor_tensor(
                    khs[:, bb % 2, :].rearrange("p (a t) -> p a t", a=NFC),
                    ktil[:, :, bb * 128:(bb + 1) * 128],
                    enb[:, :].rearrange("p (a t) -> p a t", a=NFC)
                    [:, :, bb:bb + 1].broadcast_to([128, NFC, 128]),
                    ALU.mult)),
              reads=[bf("enb")] + [bf("ktil", f) for f in range(NFC)],
              writes=[bf("khs", b % 2)])

        def emit_khT(b):
            for f in range(NFC):
                pj, pjb = (PJ0, bPJ0) if f < 4 else (PJ1, bPJ1)
                E("t", (lambda e, ff=f, p=pj, bb=b:
                        e.matmul(p[:, (ff % 4) * 128:(ff % 4) * 128 + 128],
                                 khs[:, bb % 2, ff * 128:(ff + 1) * 128],
                                 cb["identity"][:, :], start=True, stop=True)),
                  reads=[bf("khs", b % 2), bf("c", "identity")], writes=[pjb])
            for pj, pjb, f0 in ((PJ0, bPJ0, 0), (PJ1, bPJ1, 4)):
                E("s", (lambda e, p=pj, ff=f0, bb=b:
                        e.copy(khT[:, bb % 2, ff * 128:(ff + 4) * 128], p[:, :])),
                  reads=[pjb], writes=[bf("khT", b % 2)])

        if stop('xv'):
            return
        pend_bonus = []
        if ci == 0:
            emit_wwt_load(0)
            emit_wwt_load(1)
        for mf in range(NFC):
            par = mf % 2
            cpl = cplus[par]
            bcpl = bf("cplus", par)
            wbank, wbb = ((PJ0, bPJ0), (PJ1, bPJ1), (PA0, bPA0))[mf % 3]
            for kf in range(NFC):
                E("t", (lambda e, k=kf, p=par, m=mf, pb=wbank:
                        e.matmul(pb[:, :], wwt[:, 0, p, k, :], xw[:, k, :],
                                 start=(k == 0), stop=(k == NFC - 1))),
                  reads=[bf("wwt", 0, par), bf("xw", kf)], writes=[wbb])
            E("s", (lambda e, p=par, pb=wbank: e.activation(scrF[p][:, 0:512],
                                                  pb[:, :],
                                                  AF.Exp, bias=0.0, scale=-1.0)),
              reads=[wbb], writes=[bf("scrF", par, 0)])
            E("s", (lambda e, p=par: e.activation(scrF[p][:, 512:1024],
                                                  scrF[p][:, 0:512],
                                                  AF.Ln, bias=1.0, scale=1.0)),
              reads=[bf("scrF", par, 0)], writes=[bf("scrF", par, 1)])
            E("v", (lambda e, c=cpl, p=par: e.tensor_tensor_scan(
                    c[:, :], cb["reset"][:, :], scrF[p][:, 512:1024], 0.0,
                    ALU.mult, ALU.add)),
              reads=[bf("scrF", par, 1), bf("c", "reset")], writes=[bcpl])
            E("v", (lambda e, c=cpl: e.tensor_tensor(c[:, :], c[:, :],
                                                     cb["ramp"][:, :], ALU.add)),
              reads=[bf("c", "ramp")], writes=[bcpl])
            E("v", (lambda e, c=cpl, m=mf: e.tensor_scalar_mul(
                    bp[:, m * 4:(m + 1) * 4],
                    c[:, :].rearrange("p (b t) -> p b t", b=4)[:, :, 127], 0.5)),
              reads=[bcpl], writes=[bf("bp")])
            E("s", (lambda e, m=mf: e.activation(
                    enb[:, m * 4:(m + 1) * 4], bp[:, m * 4:(m + 1) * 4],
                    AF.Exp, bias=0.0, scale=-1.0)),
              reads=[bf("bp")], writes=[bf("enb")])
            E("s", (lambda e, m=mf: e.activation(
                    egb[:, m * 4:(m + 1) * 4], bp[:, m * 4:(m + 1) * 4],
                    AF.Exp, bias=0.0, scale=-2.0)),
              reads=[bf("bp")], writes=[bf("egb")])
            kbank, kbb = (PT1, bPT1) if par == 0 else (POA, bPOA)
            for kf in range(NFC):
                E("t", (lambda e, k=kf, p=par, pb=kbank:
                        e.matmul(pb[:, :], wwt[:, 1, p, k, :],
                                 xk[:, k, :], start=(k == 0), stop=(k == NFC - 1))),
                  reads=[bf("wwt", 1, par), bf("xk", kf)], writes=[kbb])
            E("s", (lambda e, m=mf, p=par, pb=kbank: e.activation(
                    ekus[:, p, 0:512], pb[:, :], AF.Exp,
                                                 bias=ppc(PP_U + m), scale=1.0)),
              reads=[kbb, bf("pp")], writes=[bf("ekus", par)])
            E("v", (lambda e, c=cpl, m=mf, p=par: e.tensor_tensor(
                    scrF[p][:, 0:512].rearrange("p (b t) -> p b t", b=4),
                    c[:, :].rearrange("p (b t) -> p b t", b=4),
                    bp[:, m * 4:(m + 1) * 4].unsqueeze(2)
                    .broadcast_to([128, 4, 128]), ALU.subtract)),
              reads=[bcpl, bf("bp")], writes=[bf("scrF", par, 0)])
            E("s", (lambda e, p=par: e.activation(
                    ekbs[:, p, 0:512], scrF[p][:, 0:512], AF.Exp,
                    bias=0.0, scale=1.0)),
              reads=[bf("scrF", par, 0)], writes=[bf("ekbs", par)])
            E("v", (lambda e, m=mf, p=par, pb=kbank: e.tensor_tensor(
                    ktil[:, m, :], pb[:, :],
                                                    ekbs[:, p, 0:512], ALU.mult)),
              reads=[kbb, bf("ekbs", par)], writes=[bf("ktil", mf)])
            rbank, rbb = (PT0, bPT0) if par == 0 else (POB, bPOB)
            for kf in range(NFC):
                E("t", (lambda e, k=kf, p=par, pb=rbank:
                        e.matmul(pb[:, :], wwt[:, 2, p, k, :],
                                 xr[:, k, :], start=(k == 0), stop=(k == NFC - 1))),
                  reads=[bf("wwt", 2, par), bf("xr", kf)], writes=[rbb])
            if mf + 2 < NFC:
                emit_wwt_load(mf + 2)
            E("s", (lambda e, p=par, pb=rbank: e.activation(
                    scrF[2 + p][:, 0:512], pb[:, :], AF.Exp,
                                                  bias=0.0, scale=-1.0)),
              reads=[rbb], writes=[bf("scrF", 2 + par, 0)])
            E("s", (lambda e, p=par: e.activation(scrF[2 + p][:, 512:1024],
                                                  scrF[2 + p][:, 0:512],
                                                  AF.Ln, bias=1.0, scale=1.0)),
              reads=[bf("scrF", 2 + par, 0)], writes=[bf("scrF", 2 + par, 1)])
            E("s", (lambda e, m=mf, p=par: e.activation(rbuf[:, m, :],
                                                 scrF[2 + p][:, 512:1024],
                                                 AF.Exp, bias=0.0, scale=-1.0)),
              reads=[bf("scrF", 2 + par, 1)], writes=[bf("rbuf", mf)])
            E("v", (lambda e, c=cpl, p=par: e.tensor_tensor(
                    scrF[2 + p][:, 1:512], c[:, 0:511],
                    scrF[2 + p][:, 513:1024], ALU.add)),
              reads=[bcpl, bf("scrF", 2 + par, 1), bf("scrF", 2 + par, 0)],
              writes=[bf("scrF", 2 + par, 0)])
            E("v", (lambda e, p=par: e.tensor_copy(
                    scrF[2 + p][:, 0:512].rearrange("p (b t) -> p b t", b=4)[:, :, 0],
                    scrF[2 + p][:, 512:1024].rearrange("p (b t) -> p b t", b=4)[:, :, 0])),
              reads=[bf("scrF", 2 + par, 1)], writes=[bf("scrF", 2 + par, 0)])
            E("v", (lambda e, m=mf, p=par: e.tensor_tensor(
                    scrF[2 + p][:, 0:512].rearrange("p (b t) -> p b t", b=4),
                    bp[:, m * 4:(m + 1) * 4].unsqueeze(2)
                    .broadcast_to([128, 4, 128]),
                    scrF[2 + p][:, 0:512].rearrange("p (b t) -> p b t", b=4),
                    ALU.subtract)),
              reads=[bf("bp")], writes=[bf("scrF", 2 + par, 0)])
            E("s", (lambda e, m=mf, p=par: e.activation(
                    qt[:, m, :], scrF[2 + p][:, 0:512], AF.Exp,
                    bias=0.0, scale=1.0)),
              reads=[bf("scrF", 2 + par, 0)], writes=[bf("qt", mf)])
            q4 = mf % 4
            E("v", (lambda e, m=mf, p=par, q=q4: e.tensor_tensor(
                    tbs[:, q // 2, (q % 2) * 512:(q % 2) * 512 + 512],
                    rbuf[:, m, :],
                    ekus[:, p, 0:512], ALU.mult)),
              reads=[bf("rbuf", mf), bf("ekus", par)], writes=[bf("tbs", q4)])
            pend_bonus.append((mf, q4))
            if len(pend_bonus) > 2:
                m2, q2 = pend_bonus.pop(0)
                for w in range(NWIN):
                    E("t", (lambda e, m=m2, ww_=w, q=q2:
                            e.matmul(PA1[:, 448 + ww_ * 16 + 2 * m:448 + ww_ * 16 + 2 * m + 2],
                                     tbs[:, q // 2, (q % 2) * 512 + ww_ * 128:
                                         (q % 2) * 512 + (ww_ + 1) * 128],
                                     cb["headind"][:, :], start=True, stop=True)),
                      reads=[bf("tbs", q2), bf("c", "headind")], writes=[bPA1])

        if stop('mf'):
            return
        for f in range(NFC):
            E("g", (lambda e, ff=f: e.tensor_tensor(
                    xw[:, ff, :].rearrange("p (s t) -> p s t", s=2),
                    dbuf[:, ff, :].rearrange("p (s t) -> p s t", s=2),
                    xprv(ff), ALU.add)),
              reads=[bf("dbuf", f)] + rd_xhf(f),
              writes=[bf("xw", f)])

        emit_khs(0)
        emit_khs(1)
        drain_casts(8)

        E("y", lambda e: e.dma_start(out=wstr[:, :, :],
                                     in_=wbf["wv"].rearrange("(kf p) n -> p kf n", p=128)),
          reads=wbfd_reads("wv"), writes=[bf("wstr", j) for j in range(NFC)],
          inc=16)
        for w in range(NWIN):
            seq, half = divmod(w, 2)
            sl = xin_slot(ci, seq)
            for kf in range(NFC):
                for nh in range(2):
                    E("t", (lambda e, k=kf, n=nh, ww_=w, h=half,
                            p=(PT0, PT1)[nh], pb=(bPT0, bPT1)[nh]:
                            e.matmul(p[:, :],
                                     xw[:, k, h * 128 + (ww_ // 2) * 256:
                                        h * 128 + (ww_ // 2) * 256 + 128],
                                     wstr[:, k, n * 512:(n + 1) * 512],
                                     start=(k == 0), stop=(k == NFC - 1))),
                      reads=[bf("xw", kf), bf("wstr", kf)],
                      writes=[(bPT0, bPT1)[nh]])
            for nh in range(2):
                E("s", (lambda e, ww_=w, n=nh, p=(PT0, PT1)[nh]:
                        e.copy(vtok[:, ww_, n * 512:(n + 1) * 512], p[:, :])),
                  reads=[(bPT0, bPT1)[nh]], writes=[bf("vtok", w)])


        if stop('vproj'):
            return
        while pend_bonus:
            m2, q2 = pend_bonus.pop(0)
            for w in range(NWIN):
                E("t", (lambda e, m=m2, ww_=w, q=q2:
                        e.matmul(PA1[:, 448 + ww_ * 16 + 2 * m:448 + ww_ * 16 + 2 * m + 2],
                                 tbs[:, q // 2, (q % 2) * 512 + ww_ * 128:
                                     (q % 2) * 512 + (ww_ + 1) * 128],
                                 cb["headind"][:, :], start=True, stop=True)),
                  reads=[bf("tbs", q2), bf("c", "headind")], writes=[bPA1])
        E("v", lambda e: e.tensor_copy(bts[:, :], PA1[:, 448:512]),
          reads=[bPA1], writes=[bf("bts")])
        drain_casts(8)
        if ci > 0:
            emit_prelude_x(ci + 1)
        E("y", lambda e: e.dma_start(out=wstr[:, :, :],
                                     in_=wbf["wo"].rearrange("(kf p) n -> p kf n", p=128)),
          reads=wbfd_reads("wo"), writes=[bf("wstr", j) for j in range(NFC)],
          inc=16)

        pend_wo = []

        def emit_stb(bt):
            E("v", (lambda e, s=bt // 2, bb=bt: e.tensor_tensor(
                    stb[:, :].rearrange("p (a t) -> p a t", a=NFC),
                    stt_s[:, s, :, :],
                    enb[:, :].rearrange("p (a t) -> p a t", a=NFC)
                    [:, :, bb:bb + 1].broadcast_to([128, NFC, 64]),
                    ALU.mult)),
              reads=[bf("state"), bf("enb")], writes=[bf("stb")])

        emit_stb(0)

        def emit_wo(bb2, seq2, half2, sl2):
            for kf in range(NFC):
                for nh2 in range(2):
                    E("t", (lambda e, k=kf, n=nh2, p=(PT0, PT1)[nh2], bb=bb2:
                            e.matmul(p[:, :],
                                     osb[:, bb % 2, k * 128:(k + 1) * 128],
                                     wstr[:, k, n * 512:(n + 1) * 512],
                                     start=(k == 0), stop=(k == NFC - 1))),
                      reads=[bf("osb", bb2 % 2), bf("wstr", kf)],
                      writes=[(bPT0, bPT1)[nh2]])
            for nh2 in range(2):
                E("v", (lambda e, h=half2, q=sl2, n=nh2, p=(PT0, PT1)[nh2]:
                        e.tensor_tensor(
                        xin[:, q, h, n * 512:(n + 1) * 512],
                        xin[:, q, h, n * 512:(n + 1) * 512], p[:, :],
                        ALU.add)),
                  reads=[(bPT0, bPT1)[nh2]], writes=[bf("xin", sl2)])

        for b in range(NWIN):
            seq, half = divmod(b, 2)
            sl = xin_slot(ci, seq)
            bc = slice(b * 128, (b + 1) * 128)
            emit_khT(b)
            if stop(f'b{b}_kh'):
                return
            ROUNDS = [([0, 2, 4, 6], [8, 10, 12, 14], 0, 1),
                      ([1, 3, 5, 7], [9, 11, 13, 15], 2, 3)]
            for rnd, (hl0, hl1, sl0, sl1) in enumerate(ROUNDS):
                if stop(f'b{b}_Ag{rnd * 2}'):
                    return
                for pa, pab, hl in ((PA0, bPA0, hl0), (PA1, bPA1, hl1)):
                    for j, h in enumerate(hl):
                        for J in range(4):
                            E("t", (lambda e, hh=h, jj=j, p=pa, JJ=J, bb=b:
                                    e.matmul(
                                        p[32 * JJ:32 * JJ + 32,
                                          jj * 128 + 32 * JJ:(jj + 1) * 128],
                                        ktil[:, hh // 2,
                                             bb * 128 + 32 * JJ:bb * 128 + 32 * JJ + 32]
                                        [(hh % 2) * 64:(hh % 2) * 64 + 64, :],
                                        qt[:, hh // 2,
                                           bb * 128 + 32 * JJ:(bb + 1) * 128]
                                        [(hh % 2) * 64:(hh % 2) * 64 + 64, :],
                                        start=True, stop=True,
                                        tile_position=((hh % 2) * 64, 32 * JJ))),
                              reads=[bf("ktil", h // 2), bf("qt", h // 2)],
                              writes=[pab])
                if stop(f'b{b}_Amm{rnd * 2}'):
                    return
                for pa, pab, aslot, hl in ((PA0, bPA0, sl0, hl0),
                                           (PA1, bPA1, sl1, hl1)):
                    apar = hl[0] % 2
                    aoff = hl[0] // 8
                    E("g", (lambda e, s=aslot, bb=b, pr_=apar, of=aoff: e.tensor_tensor(
                            atil[:, s, :].rearrange("p (a t) -> p a t", a=4),
                            cb["idmask4"][:, :].rearrange("p (a t) -> p a t", a=4),
                            bts[:, bb * 16:(bb + 1) * 16]
                            .rearrange("p (a two) -> p two a", two=2)
                            [:, pr_, 4 * of:4 * of + 4]
                            .unsqueeze(2).broadcast_to([128, 4, 128]),
                            ALU.mult)),
                      reads=[bf("bts"), bf("c", "idmask4")],
                      writes=[bf("atil", aslot)])
                    E("v", (lambda e, s=aslot, p=pa: e.copy_predicated(
                            atil[:, s, :], cb["triu4"][:, :], p[:, :])),
                      reads=[pab, bf("c", "triu4")], writes=[bf("atil", aslot)])
                if rnd == 0:
                    E("t", (lambda e: e.matmul(PJ0[:, 0:128], cb["identity"][:, :],
                                               cb["identity"][:, :],
                                               start=True, stop=True)),
                      reads=[bf("c", "identity")], writes=[bPJ0])
            if stop(f'b{b}_A'):
                return
            if pend_wo:
                emit_wo(*pend_wo.pop(0))
            def atil_slot(h):
                return (h % 2) * 2 + (h // 8), (h % 8) // 2
            for hb in range(2):
                for j in range(4):
                    pr = hb * 4 + j
                    for par in range(2):
                        h = 2 * pr + par
                        aslot2, reg = atil_slot(h)
                        E("t", (lambda e, p=pr, pp_=par, jj=j, bb=b,
                                s=aslot2, rg=reg:
                                e.matmul(POA[(pp_) * 64:(pp_) * 64 + 64,
                                             jj * 128:(jj + 1) * 128],
                                         vtok[:, bb, :][:, (2 * p + pp_) * 64:
                                                        (2 * p + pp_) * 64 + 64],
                                         atil[:, s, rg * 128:rg * 128 + 128],
                                         start=True, stop=True,
                                         tile_position=(0, pp_ * 64))),
                          reads=[bf("vtok", b), bf("atil", aslot2)],
                          writes=[bPOA])
                for par in range(2):
                    for j in range(4):
                        pr = hb * 4 + j
                        h = 2 * pr + par
                        E("t", (lambda e, p=pr, pp_=par, jj=j, hh=h, bb=b:
                                e.matmul(POB[(pp_) * 64:(pp_) * 64 + 64,
                                             jj * 128:(jj + 1) * 128],
                                         stb[:, p * 64:(p + 1) * 64]
                                         [(pp_) * 64:(pp_) * 64 + 64, :],
                                         qt[:, hh // 2, bb * 128:(bb + 1) * 128]
                                         [(hh % 2) * 64:(hh % 2) * 64 + 64, :],
                                         start=True, stop=True,
                                         tile_position=((pp_) * 64, pp_ * 64))),
                          reads=[bf("stb"), bf("qt", h // 2)], writes=[bPOB])
                E("s", (lambda e, hh=hb, bb=b: e.copy(
                        osb[:, bb % 2, hh * 512:(hh + 1) * 512], POA[:, :])),
                  reads=[bPOA], writes=[bf("osb", b % 2)])
                E("v", (lambda e, hh=hb, bb=b: e.tensor_tensor(
                        osb[:, bb % 2, hh * 512:(hh + 1) * 512],
                        osb[:, bb % 2, hh * 512:(hh + 1) * 512],
                        POB[:, :], ALU.add)),
                  reads=[bPOB], writes=[bf("osb", b % 2)])
            if stop(f'b{b}_o'):
                return
            for pr in range(NFC):
                for par in range(2):
                    h = 2 * pr + par
                    E("t", (lambda e, p=pr, pp_=par, bb=b:
                            e.matmul(PA0[(pp_) * 64:(pp_) * 64 + 64,
                                         p * 64:(p + 1) * 64],
                                     khT[:, bb % 2, :][:, (2 * p + pp_) * 64:
                                                       (2 * p + pp_) * 64 + 64],
                                     vtok[:, bb, :][:, (2 * p + pp_) * 64:
                                                    (2 * p + pp_) * 64 + 64],
                                     start=True, stop=True,
                                     tile_position=(0, pp_ * 64))),
                      reads=[bf("khT", b % 2), bf("vtok", b)], writes=[bPA0])
            E("v", (lambda e, s=seq, bb=b: e.tensor_tensor(
                    stt_s[:, s, :, :], stt_s[:, s, :, :],
                    egb[:, :].rearrange("p (a t) -> p a t", a=NFC)
                    [:, :, bb:bb + 1].broadcast_to([128, NFC, 64]),
                    ALU.mult)),
              reads=[bf("egb")], writes=[bf("state")])
            E("v", (lambda e, s=seq: e.tensor_tensor(
                    stt_s[:, s, :, :], stt_s[:, s, :, :],
                    PA0[:, :].rearrange("p (a t) -> p a t", a=NFC), ALU.add)),
              reads=[bPA0], writes=[bf("state")])
            if b + 1 < NWIN:
                emit_stb(b + 1)
            if b + 2 < NWIN:
                emit_khs(b + 2)
            if stop(f'b{b}_state'):
                return
            for hh, (pa, pab) in enumerate(((PA1, bPA1), (POA, bPOA))):
                E("t", (lambda e, h_=hh, p=pa, bb=b: e.matmul(
                        p[:, :], cb["bdones"][:, :],
                        osb[:, bb % 2, h_ * 512:(h_ + 1) * 512],
                        start=True, stop=True)),
                  reads=[bf("osb", b % 2), bf("c", "bdones")], writes=[pab])
                E("v", (lambda e, h_=hh, p=pa, bb=b: e.scalar_tensor_tensor(
                        p1b[:, bb % 2, h_ * 512:(h_ + 1) * 512], p[:, :],
                        -1.0 / 64.0,
                        osb[:, bb % 2, h_ * 512:(h_ + 1) * 512],
                        ALU.mult, ALU.add)),
                  reads=[pab, bf("osb", b % 2)], writes=[bf("p1b", b % 2)])
            E("s", (lambda e, bb=b: e.activation(gsq[:, bb % 2, :],
                                                 p1b[:, bb % 2, :], AF.Square,
                                          bias=0.0, scale=1.0)),
              reads=[bf("p1b", b % 2)], writes=[bf("gsq", b % 2)])
            for hh, (pa, pab) in enumerate(((PA1, bPA1), (POA, bPOA))):
                E("t", (lambda e, h_=hh, p=pa, bb=b: e.matmul(
                        p[:, :], cb["bdones"][:, :],
                        gsq[:, bb % 2, h_ * 512:(h_ + 1) * 512],
                        start=True, stop=True)),
                  reads=[bf("gsq", b % 2), bf("c", "bdones")], writes=[pab])
                E("s", (lambda e, h_=hh, p=pa, bb=b: e.activation(
                        scrF[2 + bb % 2][:, h_ * 512:(h_ + 1) * 512],
                        p[:, :], AF.Ln,
                        bias=ppc(PP_EPS), scale=1.0 / 64.0)),
                  reads=[pab, bf("pp")], writes=[bf("scrF", 2 + b % 2, 0),
                                                 bf("scrF", 2 + b % 2, 1)])
            E("s", (lambda e, bb=b: e.activation(rstdb[:, :],
                                                 scrF[2 + bb % 2][:, :], AF.Exp,
                                          bias=0.0, scale=-0.5)),
              reads=[bf("scrF", 2 + b % 2, 0), bf("scrF", 2 + b % 2, 1)],
              writes=[bf("rstdb")])
            E("v", (lambda e, bb=b: e.tensor_tensor(p1b[:, bb % 2, :],
                                             p1b[:, bb % 2, :], rstdb[:, :],
                                             ALU.mult)),
              reads=[bf("rstdb")], writes=[bf("p1b", b % 2)])
            E("v", (lambda e, bb=b: e.tensor_scalar(p1b[:, bb % 2, :],
                                             p1b[:, bb % 2, :],
                                             ppc(PP_LNW), ppc(PP_LNB),
                                             ALU.mult, ALU.add)),
              reads=[bf("pp")], writes=[bf("p1b", b % 2)])
            E("v", (lambda e, bb=b: e.tensor_tensor(
                    osb[:, bb % 2, :].rearrange("p (a t) -> p a t", a=NFC),
                    p1b[:, bb % 2, :].rearrange("p (a t) -> p a t", a=NFC),
                    rbuf[:, :, bb * 128:(bb + 1) * 128], ALU.mult)),
              reads=[bf("p1b", b % 2)] + [bf("rbuf", f) for f in range(NFC)],
              writes=[bf("osb", b % 2)])
            if stop(f'b{b}_gn'):
                return
            pend_wo.append((b, seq, half, sl))
            if b < 2:
                drain_casts(8)

        emit_prelude_x(ci + 1)
        ln2_srcb = lambda seq: bf("xin", xin_slot(ci, seq))
        ln2_srca = lambda seq, half: xin[:, xin_slot(ci, seq), half, :]
        emit_ln_stats(ln2_srcb, ln2_srca, 0)
        emit_ln_stats(ln2_srcb, ln2_srca, 1)
        if ci + 1 < nchunk:
            for w_ in range(NWIN):
                emit_ln_stats(
                    lambda seq, c=ci + 1: bf("xin", xin_slot(c, seq)),
                    lambda seq, half, c=ci + 1: xin[:, xin_slot(c, seq), half, :],
                    w_, soff=4)
            ln1_stats_done.add(ci + 1)
        while pend_wo:
            emit_wo(*pend_wo.pop(0))
        if stop('blocks'):
            return
        if dbg and ci == 0:
            do = dout("x1", (128, NWIN, D))
            for seq in range(NSEQ):
                sl = xin_slot(ci, seq)
                E("y", (lambda e, s=seq, q=sl, d=do:
                        e.dma_start(out=d[:, 2 * s:2 * s + 2, :], in_=xin[:, q, :, :])),
                  reads=[bf("xin", sl)], inc=16)

        emit_ln_transpose(ln2_srcb, ln2_srca, "ln2", nstats_done=2)

        if stop('ln2'):
            return
        E("v", lambda e: e.tensor_copy(xhf[:, :, :, 0:1], bx2[:, :, :].unsqueeze(3)),
          reads=[bf("bx2")], writes=[bf("xhf_b")])
        E("v", (lambda e: e.tensor_tensor(
                dbuf[:, :, :].rearrange("p a (s t) -> p a s t", s=2),
                xhf[:, :, :, 1:257], xhf[:, :, :, 0:256], ALU.subtract)),
          reads=[bf("xhf", f) for f in range(NFC)] + [bf("xhf_b")],
          writes=[bf("dbuf", f) for f in range(NFC)])
        for dst, col, db, eng in ((xr, PP_MUR, "xr", "g"),
                                  (xk, PP_MUK, "xk", "g")):
            for f in range(NFC):
                lerp(f, dst, col, db, eng)
        E("v", lambda e: e.tensor_copy(bx2[:, :, :], xhf[:, :, :, 256:257].squeeze(3)),
          reads=[bf("xhf", 0), bf("xhf", 1), bf("xhf", 2), bf("xhf", 3),
                 bf("xhf", 4), bf("xhf", 5), bf("xhf", 6), bf("xhf", 7)],
          writes=[bf("bx2")])

        if stop('ffn_lerp'):
            return
        if ci + 1 < nchunk:
            emit_prelude(ci + 1)
        def rq(k, nn):
            return (rbuf[:, 2 * k + nn, :] if k < 4
                    else qt[:, 2 * (k - 4) + nn, :])

        def rq_buf(k, nn):
            return (bf("rbuf", 2 * k + nn) if k < 4
                    else bf("qt", 2 * (k - 4) + nn))

        def load_rq(nm):
            E("y", (lambda e, d=wbf[nm]: e.dma_start(
                    out=rbuf[:, :, :].rearrange("p (kf u) c -> p kf (u c)", u=2),
                    in_=d.rearrange("(kf p) n -> p kf n", p=128)[:, 0:4, :])),
              reads=wbfd_reads(nm),
              writes=[bf("rbuf", j) for j in range(NFC)], inc=16)
            E("y", (lambda e, d=wbf[nm]: e.dma_start(
                    out=qt[:, :, :].rearrange("p (kf u) c -> p kf (u c)", u=2),
                    in_=d.rearrange("(kf p) n -> p kf n", p=128)[:, 4:8, :])),
              reads=wbfd_reads(nm),
              writes=[bf("qt", j) for j in range(NFC)], inc=16)

        load_rq("wr2")
        for w in range(NWIN):
            seq, half = divmod(w, 2)
            wc = slice(seq * 256 + half * 128, seq * 256 + half * 128 + 128)
            for kf in range(NFC):
                for n in range(2):
                    E("t", (lambda e, k=kf, nn=n, c=wc, p=(PT0, PT1)[n]:
                            e.matmul(p[:, :],
                                     xr[:, k, c], rq(k, nn),
                                     start=(k == 0), stop=(k == NFC - 1))),
                      reads=[bf("xr", kf), rq_buf(kf, n)],
                      writes=[(bPT0, bPT1)[n]])
            for n in range(2):
                E("s", (lambda e, nn=n, p=(PT0, PT1)[n], q=w % 2:
                        e.activation(scrF[q][:, nn * 512:(nn + 1) * 512], p[:, :],
                                     AF.Exp, bias=0.0, scale=-1.0)),
                  reads=[(bPT0, bPT1)[n]],
                  writes=[bf("scrF", w % 2, 0), bf("scrF", w % 2, 1)])
            E("s", (lambda e, q=w % 2: e.activation(scrF[2 + q][:, :],
                                          scrF[q][:, :], AF.Ln,
                                          bias=1.0, scale=1.0)),
              reads=[bf("scrF", w % 2, 0), bf("scrF", w % 2, 1)],
              writes=[bf("scrF", 2 + w % 2, 0), bf("scrF", 2 + w % 2, 1)])
            E("s", (lambda e, ww_=w, q=w % 2: e.activation(
                    dbuf[:, 2 * ww_:2 * ww_ + 2, :].rearrange("p a t -> p (a t)"),
                    scrF[2 + q][:, :], AF.Exp, bias=0.0, scale=-1.0)),
              reads=[bf("scrF", 2 + w % 2, 0), bf("scrF", 2 + w % 2, 1)],
              writes=[bf("dbuf", 2 * w), bf("dbuf", 2 * w + 1)])

        if stop('r2'):
            return
        E("y", lambda e: e.dma_start(out=wstr[:, :, :],
                                     in_=wbf["wk2"].rearrange("(kf p) n -> p kf n", p=128)),
          reads=wbfd_reads("wk2"), writes=[bf("wstr", j) for j in range(NFC)],
          inc=16)
        if ci + 1 < nchunk:
            emit_wwt_load(0)
            emit_wwt_load(1)
        for w in range(NWIN):
            seq, half = divmod(w, 2)
            wc = slice(seq * 256 + half * 128, seq * 256 + half * 128 + 128)
            for kf in range(NFC):
                for n in range(2):
                    E("t", (lambda e, k=kf, nn=n, c=wc, p=(PT0, PT1)[n]:
                            e.matmul(p[:, :],
                                     xk[:, k, c], wstr[:, k, nn * 512:(nn + 1) * 512],
                                     start=(k == 0), stop=(k == NFC - 1))),
                      reads=[bf("xk", kf), bf("wstr", kf)],
                      writes=[(bPT0, bPT1)[n]])
            for n in range(2):
                E("s", (lambda e, nn=n, p=(PT0, PT1)[n]:
                        e.copy(k2s[:, nn * 512:(nn + 1) * 512], p[:, :])),
                  reads=[(bPT0, bPT1)[n]], writes=[bf("k2s", n)])
            for f in range(NFC):
                pj, pjb = (PJ0, bPJ0) if f < 4 else (PJ1, bPJ1)
                E("t", (lambda e, ff=f, p=pj:
                        e.matmul(p[:, (ff % 4) * 128:(ff % 4) * 128 + 128],
                                 k2s[:, ff * 128:(ff + 1) * 128],
                                 cb["identity"][:, :], start=True, stop=True)),
                  reads=[bf("k2s", 0), bf("k2s", 1), bf("c", "identity")],
                  writes=[pjb])
            for pj, pjb, f0 in ((PJ0, bPJ0, 0), (PJ1, bPJ1, 4)):
                E("v", (lambda e, p=pj, ff=f0, c=wc: e.tensor_scalar_max(
                        kact[:, ff:ff + 4, c],
                        p[:, :].rearrange("p (a t) -> p a t", a=4), 0.0)),
                  reads=[pjb], writes=[bf("kact", f0), bf("kact", f0 + 2)])
                E("s", (lambda e, ff=f0, c=wc: e.activation(
                        kact[:, ff:ff + 4, c], kact[:, ff:ff + 4, c],
                        AF.Square, bias=0.0, scale=1.0)),
                  writes=[bf("kact", f0), bf("kact", f0 + 2)])

        if stop('k2'):
            return
        load_rq("wv2")
        for w in range(NWIN):
            seq, half = divmod(w, 2)
            sl = xin_slot(ci, seq)
            wc = slice(seq * 256 + half * 128, seq * 256 + half * 128 + 128)
            for kf in range(NFC):
                for n in range(2):
                    E("t", (lambda e, k=kf, nn=n, c=wc, p=(PT0, PT1)[n]:
                            e.matmul(p[:, :],
                                     kact[:, k, c], rq(k, nn),
                                     start=(k == 0), stop=(k == NFC - 1))),
                      reads=[bf("kact", (kf // 2) * 2), rq_buf(kf, n)],
                      writes=[(bPT0, bPT1)[n]])
            for n in range(2):
                E("v", (lambda e, ww_=w, nn=n, p=(PT0, PT1)[n]: e.tensor_tensor(
                        khs[:, ww_ % 2, nn * 512:(nn + 1) * 512],
                        dbuf[:, 2 * ww_ + nn, :], p[:, :], ALU.mult)),
                  reads=[bf("dbuf", 2 * w + n), (bPT0, bPT1)[n]],
                  writes=[bf("khs", w % 2)])
            E("g", (lambda e, h=half, q=sl, ww_=w: e.tensor_tensor(
                    xin[:, q, h, :], xin[:, q, h, :], khs[:, ww_ % 2, :],
                    ALU.add)),
              reads=[bf("khs", w % 2)], writes=[bf("xin", sl)])
        for seq in range(NSEQ):
            sl = xin_slot(ci, seq)
            E("y", (lambda e, s=seq, q=sl:
                    e.dma_start(out=od[s, t0:t0 + TC, :]
                                .rearrange("(w p) n -> p w n", p=128),
                                in_=xin[:, q, :, :])),
              reads=[bf("xin", sl)], inc=16)

    emit_prelude(0)
    emit_casts([(nm, kf, True) for nm in ("ww", "wk", "wr")
                for kf in range(NFC)])
    for nm in STREAMED:
        for kf in range(NFC):
            cast_q.append((nm, kf, False))
    for ci in range(nchunk):
        emit_chunk(ci)

    with ExitStack() as st:
        sems = {e: st.enter_context(nc.semaphore(f"sem_{e}"))
                for e in ENGINES if e != "y"}
        ysems = [st.enter_context(nc.semaphore(f"sem_y{i}")) for i in range(8)]
        blk = st.enter_context(nc.Block())
        ctx.realize(blk, sems, ysems)
    return nc, dbg_outs


_CACHE = {}


def kernel(**inputs):
    key = "prog"
    if key not in _CACHE:
        _CACHE[key] = build_program()
    nc, _ = _CACHE[key]
    x = np.asarray(inputs["L0_attn_x"], np.float32)
    consts = host_consts()
    pp = build_pp(inputs)
    wmap = dict(wr="L0_attn_W_r", wk="L0_attn_W_k", ww="L0_attn_W_w",
                wv="L0_attn_W_v", wo="L0_attn_W_o",
                wr2="L0_ffn_W_r", wk2="L0_ffn_W_k", wv2="L0_ffn_W_v")
    base = {}
    for nm, inm in wmap.items():
        base[nm] = np.ascontiguousarray(np.asarray(inputs[inm], np.float32).T)
    for nm, arr in consts.items():
        base["c_" + nm] = arr
    base["pp"] = pp
    in_maps = []
    for c in range(NCORE):
        m = dict(base)
        m["x"] = np.ascontiguousarray(x[2 * c:2 * c + 2])
        in_maps.append(m)
    res = run_bass_kernel_spmd(nc, in_maps, core_ids=list(range(NCORE)))
    out = np.concatenate([r["out"] for r in res.results], axis=0)
    return out


# version 30
# speedup vs baseline: 1.0033x; 1.0033x over previous
import numpy as np
import ml_dtypes
import concourse.bass as bass
import concourse.mybir as mybir
from concourse.bass_utils import run_bass_kernel_spmd
from contextlib import ExitStack

F32 = mybir.dt.float32
BF16 = mybir.dt.bfloat16
U8 = mybir.dt.uint8
AF = mybir.ActivationFunctionType
ALU = mybir.AluOpType

D = 1024
NFC = 8
NH = 16
DH = 64
TSEQ = 2048
NSEQ = 2
TC = 256
CC = NSEQ * TC
NWIN = 4
NCHUNK = TSEQ // TC
NCORE = 8
EPS = 1e-5

ENGINES = ("t", "v", "s", "g", "y")


class Tok:
    __slots__ = ("eng", "cnt")

    def __init__(self, eng, cnt):
        self.eng = eng
        self.cnt = cnt


class Buf:
    __slots__ = ("w", "rs")

    def __init__(self):
        self.w = None
        self.rs = []


class Ctx:
    def __init__(self):
        self.steps = {e: [] for e in ENGINES}
        self.cnt = {e: 0 for e in ENGINES}
        self.async_cnts = {e: set() for e in ENGINES}

    NY = 8

    def emit(self, eng, fn, reads=(), writes=(), inc=1, sync_self=False):
        if eng in ("s", "v", "g") or inc >= 16:
            sync_self = True
        deps = {}

        def add(tok):
            if tok is None:
                return
            if tok.eng == eng and tok.cnt not in self.async_cnts[eng]:
                return
            if True:
                if tok.eng == "y":
                    slot = ((tok.cnt // 16) - 1) % Ctx.NY
                    key = ("y", slot)
                else:
                    key = tok.eng
                if deps.get(key, 0) < tok.cnt:
                    deps[key] = tok.cnt
        for b in reads:
            add(b.w)
        for b in writes:
            add(b.w)
            for r in b.rs:
                add(r)
        self.steps[eng].append((fn, deps, inc))
        self.cnt[eng] += inc
        tok = Tok(eng, self.cnt[eng])
        if sync_self:
            self.async_cnts[eng].add(tok.cnt)
        for b in reads:
            b.rs.append(tok)
        for b in writes:
            b.w = tok
            b.rs = []
        return tok

    def realize(self, blk, sems, ysems):
        NY = len(ysems)
        assert NY == Ctx.NY
        waited = {e: set() for e in ENGINES}
        for e in ENGINES:
            for (fn, deps, inc) in self.steps[e]:
                for de, dc in deps.items():
                    waited[de[0] if isinstance(de, tuple) else de].add(dc)
        semval = {}
        for e in ENGINES:
            if e == "y":
                continue
            vals = {}
            sv = 0
            cnt = 0
            incs = []
            for (fn, deps, inc) in self.steps[e]:
                assert inc == 1, f"non-DMA inc on {e}"
                cnt += inc
                do_inc = cnt in waited[e]
                if do_inc:
                    sv += 1
                incs.append(1 if do_inc else 0)
                vals[cnt] = sv
            semval[e] = (vals, incs)
        ymap = {}
        yinfo = []
        cnt = 0
        for i, (fn, deps, inc) in enumerate(self.steps["y"]):
            assert inc == 16, "y engine must only carry DMAs"
            cnt += inc
            s = i % NY
            val = 16 * (i // NY + 1)
            ymap[cnt] = (s, val)
            yinfo.append((s, val))

        def wait_for_dep(eng, de, dc, wsf):
            if isinstance(de, tuple):
                s, val = ymap[dc]
                if val > wsf[("y", s)]:
                    eng.wait_ge(ysems[s], val)
                    wsf[("y", s)] = val
            else:
                wv = semval[de][0][dc]
                if wv > wsf[de]:
                    eng.wait_ge(sems[de], wv)
                    wsf[de] = wv

        def body_for(ename):
            def body(eng):
                wsf = {e: 0 for e in ENGINES if e != "y"}
                wsf[ename] = 0 if ename != "y" else wsf.get(ename, 0)
                for s in range(NY):
                    wsf[("y", s)] = 0
                for i, (fn, deps, inc) in enumerate(self.steps[ename]):
                    for de in sorted(deps, key=str):
                        wait_for_dep(eng, de, deps[de], wsf)
                    if ename == "y":
                        s, val = yinfo[i]
                        if val > 16 and val - 16 > wsf[("y", s)]:
                            eng.wait_ge(ysems[s], val - 16)
                            wsf[("y", s)] = val - 16
                    inst = fn(eng)
                    if ename == "y":
                        s, val = yinfo[i]
                        inst.then_inc(ysems[s], 16)
                    elif semval[ename][1][i]:
                        inst.then_inc(sems[ename], 1)
                if ename == "y":
                    for s in range(NY):
                        n = sum(1 for (si, _) in yinfo if si == s)
                        if n:
                            eng.wait_ge(ysems[s], 16 * n)
            return body
        blk.sync(body_for("y"))
        blk.vector(body_for("v"))
        blk.scalar(body_for("s"))
        blk.tensor(body_for("t"))
        blk.gpsimd(body_for("g"))


PP_TMR, PP_TMK, PP_TMV, PP_TMW = 0, 8, 16, 24
PP_MUR, PP_MUK, PP_U = 32, 40, 48
PP_LNW, PP_LNB = 56, 57
PP_OMTV = 58
PP_EPS = 66
PP_COLS = 67

STREAMED = ("wv", "wo", "wr2", "wk2", "wv2")


def host_consts():
    bf = ml_dtypes.bfloat16
    c = {}
    c["identity"] = np.eye(128).astype(bf)
    bd = np.zeros((128, 128), np.float32)
    bd[0:64, 0:64] = 1.0
    bd[64:128, 64:128] = 1.0
    c["bdones"] = bd.astype(bf)
    hi = np.zeros((128, 2), np.float32)
    hi[0:64, 0] = 1.0
    hi[64:128, 1] = 1.0
    c["headind"] = hi.astype(bf)
    c["triu4"] = np.ascontiguousarray(
        np.tile(np.triu(np.ones((128, 128), np.uint8), 1), (1, 4)))
    c["idmask4"] = np.ascontiguousarray(
        np.tile(np.eye(128, dtype=np.float32), (1, 4))).astype(bf)
    rs = np.ones((1, CC), np.float32)
    rs[0, ::128] = 0.0
    c["reset"] = np.ascontiguousarray(np.tile(rs, (128, 1))).astype(bf)
    rp = ((np.arange(CC) % 128) + 1).astype(np.float32) * 1e-4
    c["ramp"] = np.ascontiguousarray(np.tile(rp[None, :], (128, 1))).astype(bf)
    return c


CONST_DTYPES = dict(identity=BF16, bdones=BF16, headind=BF16, triu4=U8,
                    idmask4=BF16, reset=BF16, ramp=BF16)


def build_pp(ins):
    cols = []
    for nm in ("tm_r", "tm_k", "tm_v", "tm_w"):
        cols.append(np.asarray(ins["L0_attn_" + nm], np.float32).reshape(NFC, 128).T)
    cols.append(np.asarray(ins["L0_ffn_mu_r"], np.float32).reshape(NFC, 128).T)
    cols.append(np.asarray(ins["L0_ffn_mu_k"], np.float32).reshape(NFC, 128).T)
    cols.append(np.asarray(ins["L0_attn_u"], np.float32).reshape(NFC, 128).T)
    cols.append(np.tile(np.asarray(ins["L0_attn_ln_w"], np.float32), 2)[:, None])
    cols.append(np.tile(np.asarray(ins["L0_attn_ln_b"], np.float32), 2)[:, None])
    cols.append(1.0 - np.asarray(ins["L0_attn_tm_v"], np.float32).reshape(NFC, 128).T)
    cols.append(np.full((128, 1), EPS, np.float32))
    return np.ascontiguousarray(np.concatenate(cols, axis=1).astype(np.float32))


def build_program(nchunk=NCHUNK, dbg=False, stop_after=None):
    nc = bass.Bass("TRN2", target_bir_lowering=False, debug=False,
                   dynamic_dma_scratch_size=8192)
    ctx = Ctx()
    E = ctx.emit

    xd = nc.dram_tensor("x", [NSEQ, TSEQ, D], F32, kind="ExternalInput").ap()
    od = nc.dram_tensor("out", [NSEQ, TSEQ, D], F32, kind="ExternalOutput").ap()
    wd = {nm: nc.dram_tensor(nm, [D, D], F32, kind="ExternalInput").ap()
          for nm in ("wr", "wk", "ww") + STREAMED}
    ppd = nc.dram_tensor("pp", [128, PP_COLS], F32, kind="ExternalInput").ap()
    cd = {}
    cshape = dict(identity=(128, 128), bdones=(128, 128), headind=(128, 2),
                  triu4=(128, 512), idmask4=(128, 512),
                  reset=(128, CC), ramp=(128, CC))
    for nm, shp in cshape.items():
        cd[nm] = nc.dram_tensor("c_" + nm, list(shp), CONST_DTYPES[nm],
                                kind="ExternalInput").ap()
    wbf = {nm: nc.dram_tensor("wbf_" + nm, [D, D], BF16, kind="Internal").ap()
           for nm in STREAMED}
    wwt_d = {nm: nc.dram_tensor("wbf_t" + nm, [NFC * D, 128], BF16,
                                kind="Internal").ap()
             for nm in ("ww", "wk", "wr")}
    dbg_outs = {}

    def dout(name, shape, dtype=F32):
        t = nc.dram_tensor("dbg_" + name, list(shape), dtype,
                           kind="ExternalOutput").ap()
        dbg_outs[name] = t
        return t

    def S(name, shape, dtype):
        return nc.alloc_sbuf_tensor(name, shape, dtype)

    wwt = S("wwt", [128, 3, 2, NFC, 128], BF16)
    wstr = S("wstr", [128, NFC, D], BF16)
    xin = S("xin", [128, 4, 2, D], F32)
    xhf = S("xhf", [128, NFC, 2, 258], BF16)
    dbuf = S("dbuf", [128, NFC, CC], BF16)
    xw = S("xw", [128, NFC, CC], BF16)
    xk = S("xk", [128, NFC, CC], BF16)
    xr = S("xr", [128, NFC, CC], BF16)
    scrF = [S(f"scrF{i}", [128, D], F32) for i in range(4)]
    cplus = [S(f"cplus{i}", [128, CC], F32) for i in range(2)]
    ekbs = S("ekbs", [128, 2, D], BF16)
    ekus = S("ekus", [128, 2, D], BF16)
    tbs = S("tbs", [128, 2, D], BF16)
    khs = S("khs", [128, 2, D], BF16)
    gsq = S("gsq", [128, 2, D], BF16)
    k2s = S("k2s", [128, D], BF16)
    rbuf = S("rbuf", [128, NFC, CC], BF16)
    qt = S("qt", [128, NFC, CC], BF16)
    ktil = S("ktil", [128, NFC, CC], BF16)
    vtok = S("vtok", [128, NWIN, D], BF16)
    khT = S("khT", [128, 2, D], BF16)
    atil = S("atil", [128, 4, 512], BF16)
    osb = S("osb", [128, 2, D], BF16)
    p1b = S("p1b", [128, 2, D], BF16)
    rstdb = S("rstdb", [128, D], BF16)
    kact = S("kact", [128, NFC, CC], BF16)
    stt_s = S("stt_s", [128, NSEQ, NFC, DH], F32)
    stb = S("stb", [128, NFC * DH], BF16)
    bts = S("bts", [128, NWIN * NH], BF16)
    bnb = S("bnb", [128, 8, 16], F32)
    mvb = S("mvb", [128, 8, 2], F32)
    lnr = S("lnr", [128, 8, 2], F32)
    bp = S("bp", [128, NFC * NWIN], F32)
    nbp = S("nbp", [128, NFC * NWIN], F32)
    enb = S("enb", [128, NFC * NWIN], F32)
    egb = S("egb", [128, NFC * NWIN], F32)
    bx1 = S("bx1", [128, NFC, 2], BF16)
    bx2 = S("bx2", [128, NFC, 2], BF16)
    ppb = S("ppb", [128, PP_COLS], F32)
    cb = {nm: S("cb_" + nm, list(shp), CONST_DTYPES[nm])
          for nm, shp in cshape.items()}

    PJ0 = nc.alloc_psum_tensor("PJ0", [128, 512], F32)
    PJ1 = nc.alloc_psum_tensor("PJ1", [128, 512], F32)
    PT0 = nc.alloc_psum_tensor("PT0", [128, 512], F32)
    PT1 = nc.alloc_psum_tensor("PT1", [128, 512], F32)
    PA0 = nc.alloc_psum_tensor("PA0", [128, 512], F32)
    PA1 = nc.alloc_psum_tensor("PA1", [128, 512], F32)
    POA = nc.alloc_psum_tensor("POA", [128, 512], F32)
    POB = nc.alloc_psum_tensor("POB", [128, 512], F32)

    Bm = {}

    def bf(*key):
        if key not in Bm:
            Bm[key] = Buf()
        return Bm[key]

    bPJ0, bPJ1 = bf("PJ0"), bf("PJ1")
    bPT0, bPT1 = bf("PT0"), bf("PT1")
    bPA0, bPA1, bPOA, bPOB = bf("PA0"), bf("PA1"), bf("POA"), bf("POB")

    ppc = lambda col: ppb[:, col:col + 1]

    for nm in cshape:
        E("y", (lambda e, d=cd[nm], s=cb[nm]: e.dma_start(out=s[:, :], in_=d)),
          writes=[bf("c", nm)], inc=16)
    E("y", lambda e: e.dma_start(out=ppb[:, :], in_=ppd), writes=[bf("pp")], inc=16)

    for pt, ptb in ((PJ0, bPJ0), (PJ1, bPJ1), (PT0, bPT0), (PT1, bPT1),
                    (PA0, bPA0), (PA1, bPA1), (POA, bPOA), (POB, bPOB)):
        E("v", (lambda e, p=pt: e.memset(p[:, :], 0.0)), writes=[ptb])
    E("v", lambda e: e.memset(stt_s[:, :, :, :], 0.0), writes=[bf("state")])
    E("v", lambda e: e.memset(bx1[:, :, :], 0.0), writes=[bf("bx1")])
    E("v", lambda e: e.memset(bx2[:, :, :], 0.0), writes=[bf("bx2")])

    cast_i = [0]
    cast_q = []

    def cast_bounce(i, tiled):
        j = i % 4
        if tiled:
            return (scrF[j][:, :],
                    [bf("xinc", 4 + j), bf("scrF", j, 0), bf("scrF", j, 1)])
        return (xin[:, 2 + j // 2, j % 2, :], [bf("xinc", j)])

    def cast_in(nm, kf, tiled):
        i = cast_i[0]
        cast_i[0] += 1
        ap, keys = cast_bounce(i, tiled)
        E("y", (lambda e, d=wd[nm], k=kf, a=ap:
                e.dma_start(out=a, in_=d[k * 128:(k + 1) * 128, :])),
          writes=keys, inc=16)

    cast_j = [0]

    def cast_fin(nm, kf, tiled):
        i = cast_j[0]
        cast_j[0] += 1
        ap, keys = cast_bounce(i, tiled)
        if tiled:
            E("v", (lambda e, k=kf, a=ap:
                    e.tensor_copy(wstr[:, k, :], a)),
              reads=keys, writes=[bf("wstr", kf)])
            E("y", (lambda e, d=wwt_d[nm], k=kf:
                    e.dma_start(
                        out=d.rearrange("(mf kf p) n -> p mf kf n",
                                        mf=NFC, kf=NFC)[:, :, k, :],
                        in_=wstr[:, k, :].rearrange("p (mf n) -> p mf n",
                                                    mf=NFC))),
              reads=[bf("wstr", kf)], writes=[bf("wbfd", "t" + nm, kf)],
              inc=16)
        else:
            j = kf % 4
            kv = kact[:, 2 * j:2 * j + 2, :].rearrange("p a c -> p (a c)")
            E("v", (lambda e, t=kv, a=ap:
                    e.tensor_copy(t, a)),
              reads=keys, writes=[bf("kact", 2 * j)])
            E("y", (lambda e, d=wbf[nm], k=kf, t=kv:
                    e.dma_start(out=d[k * 128:(k + 1) * 128, :], in_=t)),
              reads=[bf("kact", 2 * j)], writes=[bf("wbfd", nm, kf)], inc=16)

    def emit_casts(units):
        for idx in range(min(4, len(units))):
            cast_in(*units[idx])
        for idx, u in enumerate(units):
            cast_fin(*u)
            if idx + 4 < len(units):
                cast_in(*units[idx + 4])

    def drain_casts(n):
        batch = cast_q[:n]
        del cast_q[:n]
        if batch:
            emit_casts(batch)

    def wbfd_reads(nm):
        return [bf("wbfd", nm, k) for k in range(NFC)]

    def xin_slot(ci, seq):
        return (2 * ci + seq) % 4

    def emit_ln_stats(get_src_buf, get_src_ap, w, soff=0):
        seq, half = divmod(w, 2)
        p_ = w % 4 + soff
        src_ap = get_src_ap(seq, half)
        srcb = get_src_buf(seq)
        E("v", (lambda e, a=src_ap, p=p_:
                e.bn_stats(bnb[:, p, 0:6], a[:, 0:512])),
          reads=[srcb], writes=[bf("bnb", p_)], sync_self=True)
        E("v", (lambda e, a=src_ap, p=p_:
                e.bn_stats(bnb[:, p, 6:12], a[:, 512:1024])),
          reads=[srcb], writes=[bf("bnb", p_)], sync_self=True)
        E("v", (lambda e, p=p_: e.bn_aggr(mvb[:, p, 0:2], bnb[:, p, 0:12])),
          reads=[bf("bnb", p_)], writes=[bf("mvb", p_)], sync_self=True)
        E("s", (lambda e, p=p_: e.activation(lnr[:, p, 0:1], mvb[:, p, 1:2],
                                      AF.Ln,
                                      bias=ppc(PP_EPS), scale=1.0)),
          reads=[bf("mvb", p_), bf("pp")], writes=[bf("lnr", p_)])
        E("s", (lambda e, p=p_: e.activation(lnr[:, p, 1:2], lnr[:, p, 0:1],
                                      AF.Exp, bias=0.0, scale=-0.5)),
          reads=[bf("lnr", p_)], writes=[bf("lnr", p_)])

    def emit_ln_transpose(get_src_buf, get_src_ap, tag, nstats_done=0, soff=0):
        def xtok(w):
            return (khT[:, 0, :], gsq[:, 0, :], osb[:, 0, :], osb[:, 1, :])[w % 4]

        def xtok_bufs(w):
            return ([bf("khT", 0)], [bf("gsq", 0)],
                    [bf("osb", 0)], [bf("osb", 1)])[w % 4]

        def emit_stats(w):
            seq, half = divmod(w, 2)
            p_ = w % 4 + soff
            src_ap = get_src_ap(seq, half)
            srcb = get_src_buf(seq)
            E("v", (lambda e, a=src_ap, p=p_:
                    e.bn_stats(bnb[:, p, 0:6], a[:, 0:512])),
              reads=[srcb], writes=[bf("bnb", p_)], sync_self=True)
            E("v", (lambda e, a=src_ap, p=p_:
                    e.bn_stats(bnb[:, p, 6:12], a[:, 512:1024])),
              reads=[srcb], writes=[bf("bnb", p_)], sync_self=True)
            E("v", (lambda e, p=p_: e.bn_aggr(mvb[:, p, 0:2], bnb[:, p, 0:12])),
              reads=[bf("bnb", p_)], writes=[bf("mvb", p_)], sync_self=True)
            E("s", (lambda e, p=p_: e.activation(lnr[:, p, 0:1], mvb[:, p, 1:2],
                                          AF.Ln,
                                          bias=ppc(PP_EPS), scale=1.0)),
              reads=[bf("mvb", p_), bf("pp")], writes=[bf("lnr", p_)])
            E("s", (lambda e, p=p_: e.activation(lnr[:, p, 1:2], lnr[:, p, 0:1],
                                          AF.Exp, bias=0.0, scale=-0.5)),
              reads=[bf("lnr", p_)], writes=[bf("lnr", p_)])

        def emit_xtok(w):
            seq, half = divmod(w, 2)
            p_ = w % 4 + soff
            src_ap = get_src_ap(seq, half)
            srcb = get_src_buf(seq)
            E("v", (lambda e, a=src_ap, ww_=w, p=p_:
                    e.tensor_scalar(xtok(ww_), a, mvb[:, p, 0:1],
                                    lnr[:, p, 1:2],
                                    ALU.subtract, ALU.mult)),
              reads=[srcb, bf("mvb", p_), bf("lnr", p_)],
              writes=xtok_bufs(w))

        def emit_trans(w):
            seq, half = divmod(w, 2)
            bks = (((POA, bPOA), (POB, bPOB)) if w % 2 == 0
                   else ((PJ0, bPJ0), (PJ1, bPJ1)))
            for f in range(NFC):
                pj, pjb = bks[0] if f < 4 else bks[1]
                E("t", (lambda e, ff=f, p=pj, ww_=w:
                        e.matmul(p[:, (ff % 4) * 128:(ff % 4) * 128 + 128],
                                 xtok(ww_)[:, ff * 128:(ff + 1) * 128],
                                 cb["identity"][:, :], start=True, stop=True)),
                  reads=xtok_bufs(w) + [bf("c", "identity")],
                  writes=[pjb])
            c0 = 1 + half * 128
            for pj, pjb, f0 in (bks[0] + (0,), bks[1] + (4,)):
                E("s", (lambda e, p=pj, ff=f0, s=seq, c=c0:
                        e.copy(xhf[:, ff:ff + 4, s, c:c + 128], p[:, :]
                               .rearrange("p (a b) -> p a b", a=4))),
                  reads=[pjb], writes=[bf("xhf", f0 + 0), bf("xhf", f0 + 1),
                                       bf("xhf", f0 + 2), bf("xhf", f0 + 3)])

        for w in range(nstats_done, NWIN):
            emit_ln_stats(get_src_buf, get_src_ap, w, soff)
        for w in range(NWIN):
            emit_xtok(w)
            emit_trans(w)

    xcur = lambda f: xhf[:, f, :, 1:257]
    xprv = lambda f: xhf[:, f, :, 0:256]

    def rd_xhf(f):
        return [bf("xhf", f), bf("xhf_b")]

    ntmp = [0]

    def lerp(f, dst, col, db, eng="g"):
        sl_t = ntmp[0] % 2
        ntmp[0] += 1
        E("v", (lambda e, ff=f, c=col, q=sl_t:
                e.tensor_scalar_mul(k2s[:, q * 512:(q + 1) * 512],
                                    dbuf[:, ff, :], ppc(c + ff))),
          reads=[bf("dbuf", f), bf("pp")], writes=[bf("k2s", sl_t)])
        E(eng, (lambda e, ff=f, t=dst, q=sl_t:
                e.tensor_tensor(
                    t[:, ff, :].rearrange("p (s t) -> p s t", s=2),
                    k2s[:, q * 512:(q + 1) * 512]
                    .rearrange("p (s t) -> p s t", s=2),
                    xprv(ff), ALU.add)),
          reads=[bf("k2s", sl_t), bf("pp")] + rd_xhf(f),
          writes=[bf(db, f)])

    def emit_wwt_load(mf):
        par = mf % 2
        for wi, nm in enumerate(("ww", "wk", "wr")):
            E("y", (lambda e, m=mf, p=par, d=wwt_d[nm], w_=wi:
                    e.dma_start(out=wwt[:, w_, p, :, :],
                                in_=d[m * 1024:(m + 1) * 1024, :]
                                .rearrange("(kf p) n -> p kf n", p=128))),
              reads=wbfd_reads("t" + nm), writes=[bf("wwt", wi, par)],
              inc=16)

    xdma_done = set()
    ln1_stats_done = set()

    def emit_prelude_x(ci):
        if ci in xdma_done or ci >= nchunk:
            return
        xdma_done.add(ci)
        t0 = ci * TC
        for seq in range(NSEQ):
            sl = xin_slot(ci, seq)
            wrt = [bf("xin", sl)]
            if sl >= 2:
                wrt += [bf("xinc", 2 * (sl - 2)), bf("xinc", 2 * (sl - 2) + 1)]
            E("y", (lambda e, s=seq, q=sl, t=t0:
                    e.dma_start(out=xin[:, q, :, :],
                                in_=xd[s, t:t + TC, :]
                                .rearrange("(w p) n -> p w n", p=128))),
              writes=wrt, inc=16)

    def emit_prelude(ci):
        t0 = ci * TC

        def stop(tag):
            return False
        emit_prelude_x(ci)

        emit_ln_transpose(
            lambda seq: bf("xin", xin_slot(ci, seq)),
            lambda seq, half: xin[:, xin_slot(ci, seq), half, :], "ln1",
            nstats_done=(4 if ci in ln1_stats_done else 0), soff=4)

        if stop('ln1'):
            return
        E("v", lambda e: e.tensor_copy(xhf[:, :, :, 0:1], bx1[:, :, :].unsqueeze(3)),
          reads=[bf("bx1")],
          writes=[bf("xhf_b")])

    def emit_dlerp():
        E("v", (lambda e: e.tensor_tensor(
                dbuf[:, :, :].rearrange("p a (s t) -> p a s t", s=2),
                xhf[:, :, :, 1:257], xhf[:, :, :, 0:256], ALU.subtract)),
          reads=[bf("xhf", f) for f in range(NFC)] + [bf("xhf_b")],
          writes=[bf("dbuf", f) for f in range(NFC)])
        for dst, col, db, eng in ((xw, PP_TMW, "xw", "v"),
                                  (xk, PP_TMK, "xk", "g"),
                                  (xr, PP_TMR, "xr", "g")):
            for f in range(NFC):
                lerp(f, dst, col, db, eng)
        for f in range(NFC):
            E("v", (lambda e, ff=f: e.tensor_scalar_mul(
                    dbuf[:, ff, :], dbuf[:, ff, :], ppc(PP_TMV + ff))),
              reads=[bf("pp")], writes=[bf("dbuf", f)])
        E("v", lambda e: e.tensor_copy(bx1[:, :, :], xhf[:, :, :, 256:257].squeeze(3)),
          reads=[bf("xhf", 0), bf("xhf", 1), bf("xhf", 2), bf("xhf", 3),
                 bf("xhf", 4), bf("xhf", 5), bf("xhf", 6), bf("xhf", 7)],
          writes=[bf("bx1")])

    def emit_chunk(ci):
        t0 = ci * TC

        def stop(tag):
            return stop_after is not None and stop_after == tag

        if ci == 0:
            emit_dlerp()

        if stop('lerp'):
            return
        drain_casts(8)
        def emit_khs(b):
            E("v", (lambda e, bb=b: e.tensor_tensor(
                    khs[:, bb % 2, :].rearrange("p (a t) -> p a t", a=NFC),
                    ktil[:, :, bb * 128:(bb + 1) * 128],
                    enb[:, :].rearrange("p (a t) -> p a t", a=NFC)
                    [:, :, bb:bb + 1].broadcast_to([128, NFC, 128]),
                    ALU.mult)),
              reads=[bf("enb")] + [bf("ktil", f) for f in range(NFC)],
              writes=[bf("khs", b % 2)])

        def emit_khT(b):
            for f in range(NFC):
                pj, pjb = (PJ0, bPJ0) if f < 4 else (PJ1, bPJ1)
                E("t", (lambda e, ff=f, p=pj, bb=b:
                        e.matmul(p[:, (ff % 4) * 128:(ff % 4) * 128 + 128],
                                 khs[:, bb % 2, ff * 128:(ff + 1) * 128],
                                 cb["identity"][:, :], start=True, stop=True)),
                  reads=[bf("khs", b % 2), bf("c", "identity")], writes=[pjb])
            for pj, pjb, f0 in ((PJ0, bPJ0, 0), (PJ1, bPJ1, 4)):
                E("s", (lambda e, p=pj, ff=f0, bb=b:
                        e.copy(khT[:, bb % 2, ff * 128:(ff + 4) * 128], p[:, :])),
                  reads=[pjb], writes=[bf("khT", b % 2)])

        if stop('xv'):
            return
        pend_bonus = []
        if ci == 0:
            emit_wwt_load(0)
            emit_wwt_load(1)
        for mf in range(NFC):
            par = mf % 2
            cpl = cplus[par]
            bcpl = bf("cplus", par)
            wbank, wbb = (PJ0, bPJ0) if par == 0 else (PJ1, bPJ1)
            for kf in range(NFC):
                E("t", (lambda e, k=kf, p=par, m=mf, pb=wbank:
                        e.matmul(pb[:, :], wwt[:, 0, p, k, :], xw[:, k, :],
                                 start=(k == 0), stop=(k == NFC - 1))),
                  reads=[bf("wwt", 0, par), bf("xw", kf)], writes=[wbb])
            E("s", (lambda e, p=par, pb=wbank: e.activation(scrF[p][:, 0:512],
                                                  pb[:, :],
                                                  AF.Exp, bias=0.0, scale=-1.0)),
              reads=[wbb], writes=[bf("scrF", par, 0)])
            E("s", (lambda e, p=par: e.activation(scrF[p][:, 512:1024],
                                                  scrF[p][:, 0:512],
                                                  AF.Ln, bias=1.0, scale=1.0)),
              reads=[bf("scrF", par, 0)], writes=[bf("scrF", par, 1)])
            E("v", (lambda e, c=cpl, p=par: e.tensor_tensor_scan(
                    c[:, :], cb["reset"][:, :], scrF[p][:, 512:1024], 0.0,
                    ALU.mult, ALU.add)),
              reads=[bf("scrF", par, 1), bf("c", "reset")], writes=[bcpl])
            E("v", (lambda e, c=cpl: e.tensor_tensor(c[:, :], c[:, :],
                                                     cb["ramp"][:, :], ALU.add)),
              reads=[bf("c", "ramp")], writes=[bcpl])
            E("v", (lambda e, c=cpl, m=mf: e.tensor_scalar_mul(
                    bp[:, m * 4:(m + 1) * 4],
                    c[:, :].rearrange("p (b t) -> p b t", b=4)[:, :, 127], 0.5)),
              reads=[bcpl], writes=[bf("bp")])
            E("s", (lambda e, m=mf: e.activation(
                    enb[:, m * 4:(m + 1) * 4], bp[:, m * 4:(m + 1) * 4],
                    AF.Exp, bias=0.0, scale=-1.0)),
              reads=[bf("bp")], writes=[bf("enb")])
            E("s", (lambda e, m=mf: e.activation(
                    egb[:, m * 4:(m + 1) * 4], bp[:, m * 4:(m + 1) * 4],
                    AF.Exp, bias=0.0, scale=-2.0)),
              reads=[bf("bp")], writes=[bf("egb")])
            kbank, kbb = (PT1, bPT1) if par == 0 else (POA, bPOA)
            for kf in range(NFC):
                E("t", (lambda e, k=kf, p=par, pb=kbank:
                        e.matmul(pb[:, :], wwt[:, 1, p, k, :],
                                 xk[:, k, :], start=(k == 0), stop=(k == NFC - 1))),
                  reads=[bf("wwt", 1, par), bf("xk", kf)], writes=[kbb])
            E("s", (lambda e, m=mf, p=par, pb=kbank: e.activation(
                    ekus[:, p, 0:512], pb[:, :], AF.Exp,
                                                 bias=ppc(PP_U + m), scale=1.0)),
              reads=[kbb, bf("pp")], writes=[bf("ekus", par)])
            E("v", (lambda e, c=cpl, m=mf, p=par: e.tensor_tensor(
                    scrF[p][:, 0:512].rearrange("p (b t) -> p b t", b=4),
                    c[:, :].rearrange("p (b t) -> p b t", b=4),
                    bp[:, m * 4:(m + 1) * 4].unsqueeze(2)
                    .broadcast_to([128, 4, 128]), ALU.subtract)),
              reads=[bcpl, bf("bp")], writes=[bf("scrF", par, 0)])
            E("s", (lambda e, p=par: e.activation(
                    ekbs[:, p, 0:512], scrF[p][:, 0:512], AF.Exp,
                    bias=0.0, scale=1.0)),
              reads=[bf("scrF", par, 0)], writes=[bf("ekbs", par)])
            E("v", (lambda e, m=mf, p=par, pb=kbank: e.tensor_tensor(
                    ktil[:, m, :], pb[:, :],
                                                    ekbs[:, p, 0:512], ALU.mult)),
              reads=[kbb, bf("ekbs", par)], writes=[bf("ktil", mf)])
            rbank, rbb = (PT0, bPT0) if par == 0 else (POB, bPOB)
            for kf in range(NFC):
                E("t", (lambda e, k=kf, p=par, pb=rbank:
                        e.matmul(pb[:, :], wwt[:, 2, p, k, :],
                                 xr[:, k, :], start=(k == 0), stop=(k == NFC - 1))),
                  reads=[bf("wwt", 2, par), bf("xr", kf)], writes=[rbb])
            if mf + 2 < NFC:
                emit_wwt_load(mf + 2)
            E("s", (lambda e, p=par, pb=rbank: e.activation(
                    scrF[2 + p][:, 0:512], pb[:, :], AF.Exp,
                                                  bias=0.0, scale=-1.0)),
              reads=[rbb], writes=[bf("scrF", 2 + par, 0)])
            E("s", (lambda e, p=par: e.activation(scrF[2 + p][:, 512:1024],
                                                  scrF[2 + p][:, 0:512],
                                                  AF.Ln, bias=1.0, scale=1.0)),
              reads=[bf("scrF", 2 + par, 0)], writes=[bf("scrF", 2 + par, 1)])
            E("s", (lambda e, m=mf, p=par: e.activation(rbuf[:, m, :],
                                                 scrF[2 + p][:, 512:1024],
                                                 AF.Exp, bias=0.0, scale=-1.0)),
              reads=[bf("scrF", 2 + par, 1)], writes=[bf("rbuf", mf)])
            E("v", (lambda e, c=cpl, p=par: e.tensor_tensor(
                    scrF[2 + p][:, 1:512], c[:, 0:511],
                    scrF[2 + p][:, 513:1024], ALU.add)),
              reads=[bcpl, bf("scrF", 2 + par, 1), bf("scrF", 2 + par, 0)],
              writes=[bf("scrF", 2 + par, 0)])
            E("v", (lambda e, p=par: e.tensor_copy(
                    scrF[2 + p][:, 0:512].rearrange("p (b t) -> p b t", b=4)[:, :, 0],
                    scrF[2 + p][:, 512:1024].rearrange("p (b t) -> p b t", b=4)[:, :, 0])),
              reads=[bf("scrF", 2 + par, 1)], writes=[bf("scrF", 2 + par, 0)])
            E("v", (lambda e, m=mf, p=par: e.tensor_tensor(
                    scrF[2 + p][:, 0:512].rearrange("p (b t) -> p b t", b=4),
                    bp[:, m * 4:(m + 1) * 4].unsqueeze(2)
                    .broadcast_to([128, 4, 128]),
                    scrF[2 + p][:, 0:512].rearrange("p (b t) -> p b t", b=4),
                    ALU.subtract)),
              reads=[bf("bp")], writes=[bf("scrF", 2 + par, 0)])
            E("s", (lambda e, m=mf, p=par: e.activation(
                    qt[:, m, :], scrF[2 + p][:, 0:512], AF.Exp,
                    bias=0.0, scale=1.0)),
              reads=[bf("scrF", 2 + par, 0)], writes=[bf("qt", mf)])
            q4 = mf % 4
            E("v", (lambda e, m=mf, p=par, q=q4: e.tensor_tensor(
                    tbs[:, q // 2, (q % 2) * 512:(q % 2) * 512 + 512],
                    rbuf[:, m, :],
                    ekus[:, p, 0:512], ALU.mult)),
              reads=[bf("rbuf", mf), bf("ekus", par)], writes=[bf("tbs", q4)])
            pend_bonus.append((mf, q4))
            if len(pend_bonus) > 2:
                m2, q2 = pend_bonus.pop(0)
                for w in range(NWIN):
                    E("t", (lambda e, m=m2, ww_=w, q=q2:
                            e.matmul(PA1[:, 448 + ww_ * 16 + 2 * m:448 + ww_ * 16 + 2 * m + 2],
                                     tbs[:, q // 2, (q % 2) * 512 + ww_ * 128:
                                         (q % 2) * 512 + (ww_ + 1) * 128],
                                     cb["headind"][:, :], start=True, stop=True)),
                      reads=[bf("tbs", q2), bf("c", "headind")], writes=[bPA1])

        if stop('mf'):
            return
        for f in range(NFC):
            E("g", (lambda e, ff=f: e.tensor_tensor(
                    xw[:, ff, :].rearrange("p (s t) -> p s t", s=2),
                    dbuf[:, ff, :].rearrange("p (s t) -> p s t", s=2),
                    xprv(ff), ALU.add)),
              reads=[bf("dbuf", f)] + rd_xhf(f),
              writes=[bf("xw", f)])

        emit_khs(0)
        emit_khs(1)
        drain_casts(8)

        E("y", lambda e: e.dma_start(out=wstr[:, :, :],
                                     in_=wbf["wv"].rearrange("(kf p) n -> p kf n", p=128)),
          reads=wbfd_reads("wv"), writes=[bf("wstr", j) for j in range(NFC)],
          inc=16)
        for w in range(NWIN):
            seq, half = divmod(w, 2)
            sl = xin_slot(ci, seq)
            for kf in range(NFC):
                for nh in range(2):
                    E("t", (lambda e, k=kf, n=nh, ww_=w, h=half,
                            p=(PT0, PT1)[nh], pb=(bPT0, bPT1)[nh]:
                            e.matmul(p[:, :],
                                     xw[:, k, h * 128 + (ww_ // 2) * 256:
                                        h * 128 + (ww_ // 2) * 256 + 128],
                                     wstr[:, k, n * 512:(n + 1) * 512],
                                     start=(k == 0), stop=(k == NFC - 1))),
                      reads=[bf("xw", kf), bf("wstr", kf)],
                      writes=[(bPT0, bPT1)[nh]])
            for nh in range(2):
                E("s", (lambda e, ww_=w, n=nh, p=(PT0, PT1)[nh]:
                        e.copy(vtok[:, ww_, n * 512:(n + 1) * 512], p[:, :])),
                  reads=[(bPT0, bPT1)[nh]], writes=[bf("vtok", w)])


        if stop('vproj'):
            return
        while pend_bonus:
            m2, q2 = pend_bonus.pop(0)
            for w in range(NWIN):
                E("t", (lambda e, m=m2, ww_=w, q=q2:
                        e.matmul(PA1[:, 448 + ww_ * 16 + 2 * m:448 + ww_ * 16 + 2 * m + 2],
                                 tbs[:, q // 2, (q % 2) * 512 + ww_ * 128:
                                     (q % 2) * 512 + (ww_ + 1) * 128],
                                 cb["headind"][:, :], start=True, stop=True)),
                  reads=[bf("tbs", q2), bf("c", "headind")], writes=[bPA1])
        E("v", lambda e: e.tensor_copy(bts[:, :], PA1[:, 448:512]),
          reads=[bPA1], writes=[bf("bts")])
        drain_casts(8)
        if ci > 0:
            emit_prelude_x(ci + 1)
        E("y", lambda e: e.dma_start(out=wstr[:, :, :],
                                     in_=wbf["wo"].rearrange("(kf p) n -> p kf n", p=128)),
          reads=wbfd_reads("wo"), writes=[bf("wstr", j) for j in range(NFC)],
          inc=16)

        pend_wo = []

        def emit_stb(bt):
            E("v", (lambda e, s=bt // 2, bb=bt: e.tensor_tensor(
                    stb[:, :].rearrange("p (a t) -> p a t", a=NFC),
                    stt_s[:, s, :, :],
                    enb[:, :].rearrange("p (a t) -> p a t", a=NFC)
                    [:, :, bb:bb + 1].broadcast_to([128, NFC, 64]),
                    ALU.mult)),
              reads=[bf("state"), bf("enb")], writes=[bf("stb")])

        emit_stb(0)

        def emit_wo(bb2, seq2, half2, sl2):
            for kf in range(NFC):
                for nh2 in range(2):
                    E("t", (lambda e, k=kf, n=nh2, p=(PT0, PT1)[nh2], bb=bb2:
                            e.matmul(p[:, :],
                                     osb[:, bb % 2, k * 128:(k + 1) * 128],
                                     wstr[:, k, n * 512:(n + 1) * 512],
                                     start=(k == 0), stop=(k == NFC - 1))),
                      reads=[bf("osb", bb2 % 2), bf("wstr", kf)],
                      writes=[(bPT0, bPT1)[nh2]])
            for nh2 in range(2):
                E("v", (lambda e, h=half2, q=sl2, n=nh2, p=(PT0, PT1)[nh2]:
                        e.tensor_tensor(
                        xin[:, q, h, n * 512:(n + 1) * 512],
                        xin[:, q, h, n * 512:(n + 1) * 512], p[:, :],
                        ALU.add)),
                  reads=[(bPT0, bPT1)[nh2]], writes=[bf("xin", sl2)])

        for b in range(NWIN):
            seq, half = divmod(b, 2)
            sl = xin_slot(ci, seq)
            bc = slice(b * 128, (b + 1) * 128)
            emit_khT(b)
            if stop(f'b{b}_kh'):
                return
            ROUNDS = [([0, 2, 4, 6], [8, 10, 12, 14], 0, 1),
                      ([1, 3, 5, 7], [9, 11, 13, 15], 2, 3)]
            for rnd, (hl0, hl1, sl0, sl1) in enumerate(ROUNDS):
                if stop(f'b{b}_Ag{rnd * 2}'):
                    return
                for pa, pab, hl in ((PA0, bPA0, hl0), (PA1, bPA1, hl1)):
                    for j, h in enumerate(hl):
                        for J in range(4):
                            E("t", (lambda e, hh=h, jj=j, p=pa, JJ=J, bb=b:
                                    e.matmul(
                                        p[32 * JJ:32 * JJ + 32,
                                          jj * 128 + 32 * JJ:(jj + 1) * 128],
                                        ktil[:, hh // 2,
                                             bb * 128 + 32 * JJ:bb * 128 + 32 * JJ + 32]
                                        [(hh % 2) * 64:(hh % 2) * 64 + 64, :],
                                        qt[:, hh // 2,
                                           bb * 128 + 32 * JJ:(bb + 1) * 128]
                                        [(hh % 2) * 64:(hh % 2) * 64 + 64, :],
                                        start=True, stop=True,
                                        tile_position=((hh % 2) * 64, 32 * JJ))),
                              reads=[bf("ktil", h // 2), bf("qt", h // 2)],
                              writes=[pab])
                if stop(f'b{b}_Amm{rnd * 2}'):
                    return
                for pa, pab, aslot, hl in ((PA0, bPA0, sl0, hl0),
                                           (PA1, bPA1, sl1, hl1)):
                    apar = hl[0] % 2
                    aoff = hl[0] // 8
                    E("g", (lambda e, s=aslot, bb=b, pr_=apar, of=aoff: e.tensor_tensor(
                            atil[:, s, :].rearrange("p (a t) -> p a t", a=4),
                            cb["idmask4"][:, :].rearrange("p (a t) -> p a t", a=4),
                            bts[:, bb * 16:(bb + 1) * 16]
                            .rearrange("p (a two) -> p two a", two=2)
                            [:, pr_, 4 * of:4 * of + 4]
                            .unsqueeze(2).broadcast_to([128, 4, 128]),
                            ALU.mult)),
                      reads=[bf("bts"), bf("c", "idmask4")],
                      writes=[bf("atil", aslot)])
                    E("v", (lambda e, s=aslot, p=pa: e.copy_predicated(
                            atil[:, s, :], cb["triu4"][:, :], p[:, :])),
                      reads=[pab, bf("c", "triu4")], writes=[bf("atil", aslot)])
                if rnd == 0:
                    E("t", (lambda e: e.matmul(PJ0[:, 0:128], cb["identity"][:, :],
                                               cb["identity"][:, :],
                                               start=True, stop=True)),
                      reads=[bf("c", "identity")], writes=[bPJ0])
            if stop(f'b{b}_A'):
                return
            if pend_wo:
                emit_wo(*pend_wo.pop(0))
            def atil_slot(h):
                return (h % 2) * 2 + (h // 8), (h % 8) // 2
            for hb in range(2):
                for j in range(4):
                    pr = hb * 4 + j
                    for par in range(2):
                        h = 2 * pr + par
                        aslot2, reg = atil_slot(h)
                        E("t", (lambda e, p=pr, pp_=par, jj=j, bb=b,
                                s=aslot2, rg=reg:
                                e.matmul(POA[(pp_) * 64:(pp_) * 64 + 64,
                                             jj * 128:(jj + 1) * 128],
                                         vtok[:, bb, :][:, (2 * p + pp_) * 64:
                                                        (2 * p + pp_) * 64 + 64],
                                         atil[:, s, rg * 128:rg * 128 + 128],
                                         start=True, stop=True,
                                         tile_position=(0, pp_ * 64))),
                          reads=[bf("vtok", b), bf("atil", aslot2)],
                          writes=[bPOA])
                for par in range(2):
                    for j in range(4):
                        pr = hb * 4 + j
                        h = 2 * pr + par
                        E("t", (lambda e, p=pr, pp_=par, jj=j, hh=h, bb=b:
                                e.matmul(POB[(pp_) * 64:(pp_) * 64 + 64,
                                             jj * 128:(jj + 1) * 128],
                                         stb[:, p * 64:(p + 1) * 64]
                                         [(pp_) * 64:(pp_) * 64 + 64, :],
                                         qt[:, hh // 2, bb * 128:(bb + 1) * 128]
                                         [(hh % 2) * 64:(hh % 2) * 64 + 64, :],
                                         start=True, stop=True,
                                         tile_position=((pp_) * 64, pp_ * 64))),
                          reads=[bf("stb"), bf("qt", h // 2)], writes=[bPOB])
                E("s", (lambda e, hh=hb, bb=b: e.copy(
                        osb[:, bb % 2, hh * 512:(hh + 1) * 512], POA[:, :])),
                  reads=[bPOA], writes=[bf("osb", b % 2)])
                E("v", (lambda e, hh=hb, bb=b: e.tensor_tensor(
                        osb[:, bb % 2, hh * 512:(hh + 1) * 512],
                        osb[:, bb % 2, hh * 512:(hh + 1) * 512],
                        POB[:, :], ALU.add)),
                  reads=[bPOB], writes=[bf("osb", b % 2)])
            if stop(f'b{b}_o'):
                return
            for pr in range(NFC):
                for par in range(2):
                    h = 2 * pr + par
                    E("t", (lambda e, p=pr, pp_=par, bb=b:
                            e.matmul(PA0[(pp_) * 64:(pp_) * 64 + 64,
                                         p * 64:(p + 1) * 64],
                                     khT[:, bb % 2, :][:, (2 * p + pp_) * 64:
                                                       (2 * p + pp_) * 64 + 64],
                                     vtok[:, bb, :][:, (2 * p + pp_) * 64:
                                                    (2 * p + pp_) * 64 + 64],
                                     start=True, stop=True,
                                     tile_position=(0, pp_ * 64))),
                      reads=[bf("khT", b % 2), bf("vtok", b)], writes=[bPA0])
            E("v", (lambda e, s=seq, bb=b: e.tensor_tensor(
                    stt_s[:, s, :, :], stt_s[:, s, :, :],
                    egb[:, :].rearrange("p (a t) -> p a t", a=NFC)
                    [:, :, bb:bb + 1].broadcast_to([128, NFC, 64]),
                    ALU.mult)),
              reads=[bf("egb")], writes=[bf("state")])
            E("v", (lambda e, s=seq: e.tensor_tensor(
                    stt_s[:, s, :, :], stt_s[:, s, :, :],
                    PA0[:, :].rearrange("p (a t) -> p a t", a=NFC), ALU.add)),
              reads=[bPA0], writes=[bf("state")])
            if b + 1 < NWIN:
                emit_stb(b + 1)
            if b + 2 < NWIN:
                emit_khs(b + 2)
            if stop(f'b{b}_state'):
                return
            for hh, (pa, pab) in enumerate(((PA1, bPA1), (POA, bPOA))):
                E("t", (lambda e, h_=hh, p=pa, bb=b: e.matmul(
                        p[:, :], cb["bdones"][:, :],
                        osb[:, bb % 2, h_ * 512:(h_ + 1) * 512],
                        start=True, stop=True)),
                  reads=[bf("osb", b % 2), bf("c", "bdones")], writes=[pab])
                E("v", (lambda e, h_=hh, p=pa, bb=b: e.scalar_tensor_tensor(
                        p1b[:, bb % 2, h_ * 512:(h_ + 1) * 512], p[:, :],
                        -1.0 / 64.0,
                        osb[:, bb % 2, h_ * 512:(h_ + 1) * 512],
                        ALU.mult, ALU.add)),
                  reads=[pab, bf("osb", b % 2)], writes=[bf("p1b", b % 2)])
            E("s", (lambda e, bb=b: e.activation(gsq[:, bb % 2, :],
                                                 p1b[:, bb % 2, :], AF.Square,
                                          bias=0.0, scale=1.0)),
              reads=[bf("p1b", b % 2)], writes=[bf("gsq", b % 2)])
            for hh, (pa, pab) in enumerate(((PA1, bPA1), (POA, bPOA))):
                E("t", (lambda e, h_=hh, p=pa, bb=b: e.matmul(
                        p[:, :], cb["bdones"][:, :],
                        gsq[:, bb % 2, h_ * 512:(h_ + 1) * 512],
                        start=True, stop=True)),
                  reads=[bf("gsq", b % 2), bf("c", "bdones")], writes=[pab])
                E("s", (lambda e, h_=hh, p=pa, bb=b: e.activation(
                        scrF[2 + bb % 2][:, h_ * 512:(h_ + 1) * 512],
                        p[:, :], AF.Ln,
                        bias=ppc(PP_EPS), scale=1.0 / 64.0)),
                  reads=[pab, bf("pp")], writes=[bf("scrF", 2 + b % 2, 0),
                                                 bf("scrF", 2 + b % 2, 1)])
            E("s", (lambda e, bb=b: e.activation(rstdb[:, :],
                                                 scrF[2 + bb % 2][:, :], AF.Exp,
                                          bias=0.0, scale=-0.5)),
              reads=[bf("scrF", 2 + b % 2, 0), bf("scrF", 2 + b % 2, 1)],
              writes=[bf("rstdb")])
            E("v", (lambda e, bb=b: e.tensor_tensor(p1b[:, bb % 2, :],
                                             p1b[:, bb % 2, :], rstdb[:, :],
                                             ALU.mult)),
              reads=[bf("rstdb")], writes=[bf("p1b", b % 2)])
            E("v", (lambda e, bb=b: e.tensor_scalar(p1b[:, bb % 2, :],
                                             p1b[:, bb % 2, :],
                                             ppc(PP_LNW), ppc(PP_LNB),
                                             ALU.mult, ALU.add)),
              reads=[bf("pp")], writes=[bf("p1b", b % 2)])
            E("v", (lambda e, bb=b: e.tensor_tensor(
                    osb[:, bb % 2, :].rearrange("p (a t) -> p a t", a=NFC),
                    p1b[:, bb % 2, :].rearrange("p (a t) -> p a t", a=NFC),
                    rbuf[:, :, bb * 128:(bb + 1) * 128], ALU.mult)),
              reads=[bf("p1b", b % 2)] + [bf("rbuf", f) for f in range(NFC)],
              writes=[bf("osb", b % 2)])
            if stop(f'b{b}_gn'):
                return
            pend_wo.append((b, seq, half, sl))
            if b < 2:
                drain_casts(8)

        emit_prelude_x(ci + 1)
        ln2_srcb = lambda seq: bf("xin", xin_slot(ci, seq))
        ln2_srca = lambda seq, half: xin[:, xin_slot(ci, seq), half, :]
        emit_ln_stats(ln2_srcb, ln2_srca, 0)
        emit_ln_stats(ln2_srcb, ln2_srca, 1)
        if ci + 1 < nchunk:
            for w_ in range(NWIN):
                emit_ln_stats(
                    lambda seq, c=ci + 1: bf("xin", xin_slot(c, seq)),
                    lambda seq, half, c=ci + 1: xin[:, xin_slot(c, seq), half, :],
                    w_, soff=4)
            ln1_stats_done.add(ci + 1)
        while pend_wo:
            emit_wo(*pend_wo.pop(0))
        if stop('blocks'):
            return
        if dbg and ci == 0:
            do = dout("x1", (128, NWIN, D))
            for seq in range(NSEQ):
                sl = xin_slot(ci, seq)
                E("y", (lambda e, s=seq, q=sl, d=do:
                        e.dma_start(out=d[:, 2 * s:2 * s + 2, :], in_=xin[:, q, :, :])),
                  reads=[bf("xin", sl)], inc=16)

        emit_ln_transpose(ln2_srcb, ln2_srca, "ln2", nstats_done=2)

        if stop('ln2'):
            return
        E("v", lambda e: e.tensor_copy(xhf[:, :, :, 0:1], bx2[:, :, :].unsqueeze(3)),
          reads=[bf("bx2")], writes=[bf("xhf_b")])
        E("v", (lambda e: e.tensor_tensor(
                dbuf[:, :, :].rearrange("p a (s t) -> p a s t", s=2),
                xhf[:, :, :, 1:257], xhf[:, :, :, 0:256], ALU.subtract)),
          reads=[bf("xhf", f) for f in range(NFC)] + [bf("xhf_b")],
          writes=[bf("dbuf", f) for f in range(NFC)])
        for dst, col, db, eng in ((xr, PP_MUR, "xr", "g"),
                                  (xk, PP_MUK, "xk", "g")):
            for f in range(NFC):
                lerp(f, dst, col, db, eng)
        E("v", lambda e: e.tensor_copy(bx2[:, :, :], xhf[:, :, :, 256:257].squeeze(3)),
          reads=[bf("xhf", 0), bf("xhf", 1), bf("xhf", 2), bf("xhf", 3),
                 bf("xhf", 4), bf("xhf", 5), bf("xhf", 6), bf("xhf", 7)],
          writes=[bf("bx2")])

        if stop('ffn_lerp'):
            return
        if ci + 1 < nchunk:
            emit_prelude(ci + 1)
        def rq(k, nn):
            return (rbuf[:, 2 * k + nn, :] if k < 4
                    else qt[:, 2 * (k - 4) + nn, :])

        def rq_buf(k, nn):
            return (bf("rbuf", 2 * k + nn) if k < 4
                    else bf("qt", 2 * (k - 4) + nn))

        def load_rq(nm):
            E("y", (lambda e, d=wbf[nm]: e.dma_start(
                    out=rbuf[:, :, :].rearrange("p (kf u) c -> p kf (u c)", u=2),
                    in_=d.rearrange("(kf p) n -> p kf n", p=128)[:, 0:4, :])),
              reads=wbfd_reads(nm),
              writes=[bf("rbuf", j) for j in range(NFC)], inc=16)
            E("y", (lambda e, d=wbf[nm]: e.dma_start(
                    out=qt[:, :, :].rearrange("p (kf u) c -> p kf (u c)", u=2),
                    in_=d.rearrange("(kf p) n -> p kf n", p=128)[:, 4:8, :])),
              reads=wbfd_reads(nm),
              writes=[bf("qt", j) for j in range(NFC)], inc=16)

        load_rq("wr2")
        for w in range(NWIN):
            seq, half = divmod(w, 2)
            wc = slice(seq * 256 + half * 128, seq * 256 + half * 128 + 128)
            for kf in range(NFC):
                for n in range(2):
                    E("t", (lambda e, k=kf, nn=n, c=wc, p=(PT0, PT1)[n]:
                            e.matmul(p[:, :],
                                     xr[:, k, c], rq(k, nn),
                                     start=(k == 0), stop=(k == NFC - 1))),
                      reads=[bf("xr", kf), rq_buf(kf, n)],
                      writes=[(bPT0, bPT1)[n]])
            for n in range(2):
                E("s", (lambda e, nn=n, p=(PT0, PT1)[n], q=w % 2:
                        e.activation(scrF[q][:, nn * 512:(nn + 1) * 512], p[:, :],
                                     AF.Exp, bias=0.0, scale=-1.0)),
                  reads=[(bPT0, bPT1)[n]],
                  writes=[bf("scrF", w % 2, 0), bf("scrF", w % 2, 1)])
            E("s", (lambda e, q=w % 2: e.activation(scrF[2 + q][:, :],
                                          scrF[q][:, :], AF.Ln,
                                          bias=1.0, scale=1.0)),
              reads=[bf("scrF", w % 2, 0), bf("scrF", w % 2, 1)],
              writes=[bf("scrF", 2 + w % 2, 0), bf("scrF", 2 + w % 2, 1)])
            E("s", (lambda e, ww_=w, q=w % 2: e.activation(
                    ktil[:, 2 * ww_:2 * ww_ + 2, :].rearrange("p a t -> p (a t)"),
                    scrF[2 + q][:, :], AF.Exp, bias=0.0, scale=-1.0)),
              reads=[bf("scrF", 2 + w % 2, 0), bf("scrF", 2 + w % 2, 1)],
              writes=[bf("ktil", 2 * w), bf("ktil", 2 * w + 1)])

        if stop('r2'):
            return
        E("y", lambda e: e.dma_start(out=wstr[:, :, :],
                                     in_=wbf["wk2"].rearrange("(kf p) n -> p kf n", p=128)),
          reads=wbfd_reads("wk2"), writes=[bf("wstr", j) for j in range(NFC)],
          inc=16)
        if ci + 1 < nchunk:
            emit_wwt_load(0)
            emit_wwt_load(1)
        for w in range(NWIN):
            seq, half = divmod(w, 2)
            wc = slice(seq * 256 + half * 128, seq * 256 + half * 128 + 128)
            for kf in range(NFC):
                for n in range(2):
                    E("t", (lambda e, k=kf, nn=n, c=wc, p=(PT0, PT1)[n]:
                            e.matmul(p[:, :],
                                     xk[:, k, c], wstr[:, k, nn * 512:(nn + 1) * 512],
                                     start=(k == 0), stop=(k == NFC - 1))),
                      reads=[bf("xk", kf), bf("wstr", kf)],
                      writes=[(bPT0, bPT1)[n]])
            for n in range(2):
                E("s", (lambda e, nn=n, p=(PT0, PT1)[n]:
                        e.copy(k2s[:, nn * 512:(nn + 1) * 512], p[:, :])),
                  reads=[(bPT0, bPT1)[n]], writes=[bf("k2s", n)])
            for f in range(NFC):
                pj, pjb = (PJ0, bPJ0) if f < 4 else (PJ1, bPJ1)
                E("t", (lambda e, ff=f, p=pj:
                        e.matmul(p[:, (ff % 4) * 128:(ff % 4) * 128 + 128],
                                 k2s[:, ff * 128:(ff + 1) * 128],
                                 cb["identity"][:, :], start=True, stop=True)),
                  reads=[bf("k2s", 0), bf("k2s", 1), bf("c", "identity")],
                  writes=[pjb])
            for pj, pjb, f0 in ((PJ0, bPJ0, 0), (PJ1, bPJ1, 4)):
                E("v", (lambda e, p=pj, ff=f0, c=wc: e.tensor_scalar_max(
                        kact[:, ff:ff + 4, c],
                        p[:, :].rearrange("p (a t) -> p a t", a=4), 0.0)),
                  reads=[pjb], writes=[bf("kact", f0), bf("kact", f0 + 2)])
                E("s", (lambda e, ff=f0, c=wc: e.activation(
                        kact[:, ff:ff + 4, c], kact[:, ff:ff + 4, c],
                        AF.Square, bias=0.0, scale=1.0)),
                  writes=[bf("kact", f0), bf("kact", f0 + 2)])

        if ci + 1 < nchunk:
            emit_dlerp()
        if stop('k2'):
            return
        load_rq("wv2")
        for w in range(NWIN):
            seq, half = divmod(w, 2)
            sl = xin_slot(ci, seq)
            wc = slice(seq * 256 + half * 128, seq * 256 + half * 128 + 128)
            for kf in range(NFC):
                for n in range(2):
                    E("t", (lambda e, k=kf, nn=n, c=wc, p=(PT0, PT1)[n]:
                            e.matmul(p[:, :],
                                     kact[:, k, c], rq(k, nn),
                                     start=(k == 0), stop=(k == NFC - 1))),
                      reads=[bf("kact", (kf // 2) * 2), rq_buf(kf, n)],
                      writes=[(bPT0, bPT1)[n]])
            for n in range(2):
                E("v", (lambda e, ww_=w, nn=n, p=(PT0, PT1)[n]: e.tensor_tensor(
                        khs[:, ww_ % 2, nn * 512:(nn + 1) * 512],
                        ktil[:, 2 * ww_ + nn, :], p[:, :], ALU.mult)),
                  reads=[bf("ktil", 2 * w + n), (bPT0, bPT1)[n]],
                  writes=[bf("khs", w % 2)])
            E("g", (lambda e, h=half, q=sl, ww_=w: e.tensor_tensor(
                    xin[:, q, h, :], xin[:, q, h, :], khs[:, ww_ % 2, :],
                    ALU.add)),
              reads=[bf("khs", w % 2)], writes=[bf("xin", sl)])
        for seq in range(NSEQ):
            sl = xin_slot(ci, seq)
            E("y", (lambda e, s=seq, q=sl:
                    e.dma_start(out=od[s, t0:t0 + TC, :]
                                .rearrange("(w p) n -> p w n", p=128),
                                in_=xin[:, q, :, :])),
              reads=[bf("xin", sl)], inc=16)

    emit_prelude(0)
    emit_casts([(nm, kf, True) for nm in ("ww", "wk", "wr")
                for kf in range(NFC)])
    for nm in STREAMED:
        for kf in range(NFC):
            cast_q.append((nm, kf, False))
    for ci in range(nchunk):
        emit_chunk(ci)

    with ExitStack() as st:
        sems = {e: st.enter_context(nc.semaphore(f"sem_{e}"))
                for e in ENGINES if e != "y"}
        ysems = [st.enter_context(nc.semaphore(f"sem_y{i}")) for i in range(8)]
        blk = st.enter_context(nc.Block())
        ctx.realize(blk, sems, ysems)
    return nc, dbg_outs


_CACHE = {}


def kernel(**inputs):
    key = "prog"
    if key not in _CACHE:
        _CACHE[key] = build_program()
    nc, _ = _CACHE[key]
    x = np.asarray(inputs["L0_attn_x"], np.float32)
    consts = host_consts()
    pp = build_pp(inputs)
    wmap = dict(wr="L0_attn_W_r", wk="L0_attn_W_k", ww="L0_attn_W_w",
                wv="L0_attn_W_v", wo="L0_attn_W_o",
                wr2="L0_ffn_W_r", wk2="L0_ffn_W_k", wv2="L0_ffn_W_v")
    base = {}
    for nm, inm in wmap.items():
        base[nm] = np.ascontiguousarray(np.asarray(inputs[inm], np.float32).T)
    for nm, arr in consts.items():
        base["c_" + nm] = arr
    base["pp"] = pp
    in_maps = []
    for c in range(NCORE):
        m = dict(base)
        m["x"] = np.ascontiguousarray(x[2 * c:2 * c + 2])
        in_maps.append(m)
    res = run_bass_kernel_spmd(nc, in_maps, core_ids=list(range(NCORE)))
    out = np.concatenate([r["out"] for r in res.results], axis=0)
    return out
